# version 01
# baseline (speedup 1.0000x reference)
import math
from contextlib import ExitStack

import numpy as np
import ml_dtypes

import concourse.bass as bass
import concourse.mybir as mybir
from concourse.bass_utils import run_bass_kernel_spmd

F32 = mybir.dt.float32
BF16 = mybir.dt.bfloat16
AF = mybir.ActivationFunctionType
AX = mybir.AxisListType
OP = mybir.AluOpType

NCORE = 8
D = 1024
NCH = 8
TPC = 2048
S = 4096
B = 4
NEG = -3.0e38
EPS = 1e-5

_prog_cache = {}


class Sems:

    def __init__(self, nc, es, names):
        self.h = {n: es.enter_context(nc.semaphore(n)) for n in names}
        self.n = {n: 0 for n in names}

    def inc(self, name, k=1):
        self.n[name] += k
        return self.h[name]

    def __getitem__(self, name):
        return self.h[name]


def _ceil(a, b):
    return -(-a // b)


def _ss(eng, sm, ins, name="dve"):
    ins.then_inc(sm.inc(name), 1)
    eng.wait_ge(sm[name], sm.n[name])



def emit_ln_stats(block, sm, *, ps_st, ones_col, src_stripe, sq_buf, nstripe,
                  stripe_w=512, nch=NCH, wait=None):
    pe_base = sm.n["pe"]
    act_base = sm.n["act"]
    for s in range(nstripe):
        for c in range(nch):
            idx = s * nch + c

            @block.scalar
            def _(scalar, s=s, c=c, idx=idx):
                if idx == 0 and wait is not None:
                    wait(scalar)
                if idx >= 2:
                    scalar.wait_ge(sm["pe"], pe_base + idx - 1)
                ins = scalar.activation(
                    sq_buf[:, (idx % 2) * stripe_w:(idx % 2 + 1) * stripe_w],
                    src_stripe(c, s), AF.Square)
                ins.then_inc(sm.inc("act"), 1)

            @block.tensor
            def _(tensor, s=s, c=c, idx=idx):
                tensor.wait_ge(sm["act"], act_base + idx + 1)
                tensor.matmul(ps_st(s)[0:1, :], ones_col[:, 0:1], src_stripe(c, s),
                              start=(c == 0), stop=(c == nch - 1))
                ins = tensor.matmul(
                    ps_st(s)[32:33, :], ones_col[:, 0:1],
                    sq_buf[:, (idx % 2) * stripe_w:(idx % 2 + 1) * stripe_w],
                    start=(c == 0), stop=(c == nch - 1))
                ins.then_inc(sm.inc("pe"), 1)


def emit_ln_rows(block, sm, *, ps_st, a_row, b_row, srow, nstripe,
                 stripe_w=512, nd=D, eps=EPS):
    for s in range(nstripe):
        sl = slice(s * stripe_w, (s + 1) * stripe_w)
        v = srow[0:1, 0:stripe_w]
        r = srow[0:1, stripe_w:2 * stripe_w]
        t = srow[0:1, 2 * stripe_w:3 * stripe_w]

        @block.vector
        def _(vector, s=s, sl=sl, v=v, t=t):
            vector.wait_ge(sm["pe"], sm.n["pe"])
            if s > 0:
                vector.wait_ge(sm["act"], sm.n["act"])
            st = ps_st(s)
            _ss(vector, sm, vector.tensor_scalar_mul(b_row[0:1, sl], st[0:1, :],
                                                     -1.0 / nd))
            _ss(vector, sm, vector.tensor_scalar(v, st[32:33, :], 1.0 / nd, eps,
                                                 OP.mult, OP.add))
            _ss(vector, sm, vector.tensor_mul(t, b_row[0:1, sl], b_row[0:1, sl]))
            ins = vector.tensor_sub(v, v, t)
            ins.then_inc(sm.inc("dve"), 1)

        @block.scalar
        def _(scalar, t=t, v=v):
            scalar.wait_ge(sm["dve"], sm.n["dve"])
            ins = scalar.activation(t, v, AF.Sqrt)
            ins.then_inc(sm.inc("act"), 1)

        @block.vector
        def _(vector, sl=sl, v=v, r=r, t=t):
            vector.wait_ge(sm["act"], sm.n["act"])
            _ss(vector, sm, vector.reciprocal(r, t))
            _ss(vector, sm, vector.tensor_mul(t, v, r))
            _ss(vector, sm, vector.tensor_mul(t, t, r))
            _ss(vector, sm, vector.tensor_scalar(t, t, -0.5, 1.5, OP.mult, OP.add))
            ins = vector.tensor_mul(a_row[0:1, sl], r, t)
            ins.then_inc(sm.inc("dve"), 1)


def emit_ln_apply(block, sm, *, v_chunk, a_row, b_row, ones1, ps_bc, out_chunk,
                  x_chunk=None, tmp, nstripe, stripe_w=512, nch=NCH):
    for s in range(nstripe):
        sl = slice(s * stripe_w, (s + 1) * stripe_w)

        @block.tensor
        def _(tensor, s=s, sl=sl):
            tensor.wait_ge(sm["dve"], sm.n["dve"])
            tensor.matmul(ps_bc[:, 0:512], ones1[0:1, :], a_row[0:1, sl],
                          start=True, stop=True)
            ins = tensor.matmul(ps_bc[:, 512:1024], ones1[0:1, :], b_row[0:1, sl],
                                start=True, stop=True)
            ins.then_inc(sm.inc("pe"), 1)

        @block.vector
        def _(vector, s=s, sl=sl):
            vector.wait_ge(sm["pe"], sm.n["pe"])
            for c in range(nch):
                _ss(vector, sm, vector.tensor_add(tmp[:, 0:stripe_w],
                                                  v_chunk(c)[:, sl],
                                                  ps_bc[:, 512:1024]))
                if x_chunk is None:
                    ins = vector.tensor_mul(out_chunk(c)[:, sl], tmp[:, 0:stripe_w],
                                            ps_bc[:, 0:512])
                    if c < nch - 1:
                        _ss(vector, sm, ins)
                else:
                    _ss(vector, sm, vector.tensor_mul(tmp[:, 0:stripe_w],
                                                      tmp[:, 0:stripe_w],
                                                      ps_bc[:, 0:512]))
                    ins = vector.tensor_add(out_chunk(c)[:, sl], x_chunk(c)[:, sl],
                                            tmp[:, 0:stripe_w])
                    if c < nch - 1:
                        _ss(vector, sm, ins)
            ins.then_inc(sm.inc("dve"), 1)


def emit_dense(block, sm, *, xop, kdim, T_, wout, wdram, wbuf, ps_tile,
               evac, evac_sem, bw=512, stripe_w=512, wait_x=None,
               wrows0=0, nbank=4):
    nk = _ceil(kdim, 128)
    ns = T_ // stripe_w
    nob = _ceil(wout, bw)
    pe_after_batch = []
    tile_i = [0]
    evac_base = sm.n[evac_sem]
    wl_name = ["wla", "wlb"]
    wl_tgt = {}

    for ob in range(nob):
        obw = min(bw, wout - ob * bw)
        pb = ob % 2

        @block.gpsimd
        def _(gpsimd, ob=ob, obw=obw, pb=pb):
            if ob >= 2:
                gpsimd.wait_ge(sm["pe"], pe_after_batch[ob - 2])
            for kc in range(nk):
                klen = min(128, kdim - kc * 128)
                ins = gpsimd.dma_start(
                    out=wbuf[0:klen, (pb * nk + kc) * bw:(pb * nk + kc) * bw + obw],
                    in_=wdram[wrows0 + kc * 128:wrows0 + kc * 128 + klen,
                              ob * bw:ob * bw + obw])
                ins.then_inc(sm.inc(wl_name[pb], 16), 16)
        wl_tgt[ob] = sm.n[wl_name[pb]]

        noc = _ceil(obw, 128)
        for oc in range(noc):
            ow = min(128, obw - oc * 128)
            for s in range(ns):
                ti = tile_i[0]
                tile_i[0] += 1

                @block.tensor
                def _(tensor, ob=ob, pb=pb, oc=oc, ow=ow, s=s, ti=ti):
                    if ti == 0 and wait_x is not None:
                        wait_x(tensor)
                    tensor.wait_ge(sm[wl_name[pb]], wl_tgt[ob])
                    if ti >= nbank:
                        tensor.wait_ge(sm[evac_sem], evac_base + ti - nbank + 1)
                    pst = ps_tile(ti % nbank)
                    for kc in range(nk):
                        klen = min(128, kdim - kc * 128)
                        ins = tensor.matmul(
                            pst[0:ow, :],
                            wbuf[0:klen, (pb * nk + kc) * bw + oc * 128:
                                 (pb * nk + kc) * bw + oc * 128 + ow],
                            xop(kc)[:, s * stripe_w:(s + 1) * stripe_w],
                            start=(kc == 0), stop=(kc == nk - 1))
                    ins.then_inc(sm.inc("pe"), 1)

                evac(block, ob * bw + oc * 128, ow, s, ps_tile(ti % nbank))
            if oc == noc - 1:
                pe_after_batch.append(sm.n["pe"])



def build_glue(emit_h):
    nc = bass.Bass("TRN2", target_bir_lowering=False)
    x_d = nc.dram_tensor("x", [D, TPC], F32, kind="ExternalInput").ap()
    bo_d = nc.dram_tensor("bo", [D, TPC], F32, kind="ExternalInput").ap()
    xn_d = nc.dram_tensor("xn", [D, TPC], F32, kind="ExternalOutput").ap()
    if emit_h:
        hn_d = nc.dram_tensor("hn", [D, TPC], F32, kind="ExternalOutput").ap()
    NS = TPC // 512
    es = ExitStack()
    with es:
        x_sb = es.enter_context(nc.sbuf_tensor("x_sb", [128, NCH * TPC], F32))
        bo_sb = es.enter_context(nc.sbuf_tensor("bo_sb", [128, NCH * TPC], F32))
        sq = es.enter_context(nc.sbuf_tensor("sq", [128, 2 * 512], F32))
        a_row = es.enter_context(nc.sbuf_tensor("a_row", [1, TPC], F32))
        b_row = es.enter_context(nc.sbuf_tensor("b_row", [1, TPC], F32))
        srow = es.enter_context(nc.sbuf_tensor("srow", [1, 3 * 512], F32))
        tmp = es.enter_context(nc.sbuf_tensor("tmp", [128, 512], F32))
        ones_col = es.enter_context(nc.sbuf_tensor("ones_col", [128, 1], F32))
        epst = es.enter_context(nc.sbuf_tensor("epst", [1, 1], F32))
        ones1 = es.enter_context(nc.sbuf_tensor("ones1", [1, 128], F32))
        ps = es.enter_context(nc.psum_tensor("ps", [128, 8 * 512], F32))
        sm = Sems(nc, es, ["ld", "pe", "act", "dve", "st"])
        block = es.enter_context(nc.Block())

        xc = lambda c: x_sb[:, c * TPC:(c + 1) * TPC]
        boc = lambda c: bo_sb[:, c * TPC:(c + 1) * TPC]
        ps_st = lambda s: ps[:, s * 512:(s + 1) * 512]
        ps_bc = ps[:, 4 * 512:6 * 512]

        @block.sync
        def _(sync):
            sync.dma_start(out=x_sb.ap().rearrange("p (c t) -> p c t", t=TPC),
                           in_=x_d.rearrange("(c p) t -> p c t", p=128)).then_inc(sm.inc("ld", 16), 16)
            sync.dma_start(out=bo_sb.ap().rearrange("p (c t) -> p c t", t=TPC),
                           in_=bo_d.rearrange("(c p) t -> p c t", p=128)).then_inc(sm.inc("ld", 16), 16)

        @block.vector
        def _(vector):
            vector.memset(ones_col[:], 1.0)
            vector.memset(epst[:], EPS)
            ins = vector.memset(ones1[:], 1.0)
            ins.then_inc(sm.inc("dve"), 1)

        def wait_loads(eng):
            eng.wait_ge(sm["ld"], sm.n["ld"])
            eng.wait_ge(sm["dve"], 1)

        emit_ln_stats(block, sm, ps_st=ps_st, ones_col=ones_col,
                      src_stripe=lambda c, s: boc(c)[:, s * 512:(s + 1) * 512],
                      sq_buf=sq, nstripe=NS, wait=wait_loads)
        emit_ln_rows(block, sm, ps_st=ps_st, a_row=a_row, b_row=b_row,
                     srow=srow, nstripe=NS)
        emit_ln_apply(block, sm, v_chunk=boc, a_row=a_row, b_row=b_row,
                      ones1=ones1, ps_bc=ps_bc, out_chunk=xc, x_chunk=xc,
                      tmp=tmp, nstripe=NS)

        @block.sync
        def _(sync):
            sync.wait_ge(sm["dve"], sm.n["dve"])
            sync.dma_start(out=xn_d.rearrange("(c p) t -> p c t", p=128),
                           in_=x_sb.ap().rearrange("p (c t) -> p c t", t=TPC)).then_inc(sm.inc("st", 16), 16)

        if emit_h:
            ndve_now = sm.n["dve"]
            emit_ln_stats(block, sm, ps_st=ps_st, ones_col=ones_col,
                          src_stripe=lambda c, s: xc(c)[:, s * 512:(s + 1) * 512],
                          sq_buf=sq, nstripe=NS,
                          wait=lambda eng: eng.wait_ge(sm["dve"], ndve_now))
            emit_ln_rows(block, sm, ps_st=ps_st, a_row=a_row, b_row=b_row,
                         srow=srow, nstripe=NS)
            emit_ln_apply(block, sm, v_chunk=xc, a_row=a_row, b_row=b_row,
                          ones1=ones1, ps_bc=ps_bc, out_chunk=boc, x_chunk=None,
                          tmp=tmp, nstripe=NS)

            @block.sync
            def _(sync):
                sync.wait_ge(sm["dve"], sm.n["dve"])
                sync.dma_start(out=hn_d.rearrange("(c p) t -> p c t", p=128),
                               in_=bo_sb.ap().rearrange("p (c t) -> p c t", t=TPC)).then_inc(sm.inc("st", 16), 16)

        @block.sync
        def _(sync):
            sync.wait_ge(sm["st"], sm.n["st"])
    return nc



def build_projglue():
    nc = bass.Bass("TRN2", target_bir_lowering=False)
    x_d = nc.dram_tensor("x", [D, TPC], F32, kind="ExternalInput").ap()
    at_d = nc.dram_tensor("attn", [D, TPC], F32, kind="ExternalInput").ap()
    wo_d = nc.dram_tensor("wo", [D, D], F32, kind="ExternalInput").ap()
    xn_d = nc.dram_tensor("xn", [D, TPC], F32, kind="ExternalOutput").ap()
    hn_d = nc.dram_tensor("hn", [D, TPC], F32, kind="ExternalOutput").ap()
    NS = TPC // 512
    es = ExitStack()
    with es:
        x_sb = es.enter_context(nc.sbuf_tensor("x_sb", [128, NCH * TPC], F32))
        bo_sb = es.enter_context(nc.sbuf_tensor("bo_sb", [128, NCH * TPC], F32))
        at_sb = es.enter_context(nc.sbuf_tensor("at_sb", [128, NCH * TPC], BF16))
        wbuf = es.enter_context(nc.sbuf_tensor("wbuf", [128, 2 * NCH * 512], BF16))
        sq = es.enter_context(nc.sbuf_tensor("sq", [128, 2 * 512], F32))
        a_row = es.enter_context(nc.sbuf_tensor("a_row", [1, TPC], F32))
        b_row = es.enter_context(nc.sbuf_tensor("b_row", [1, TPC], F32))
        srow = es.enter_context(nc.sbuf_tensor("srow", [1, 3 * 512], F32))
        tmp = es.enter_context(nc.sbuf_tensor("tmp", [128, 512], F32))
        ones_col = es.enter_context(nc.sbuf_tensor("ones_col", [128, 1], F32))
        ones1 = es.enter_context(nc.sbuf_tensor("ones1", [1, 128], F32))
        ps = es.enter_context(nc.psum_tensor("ps", [128, 8 * 512], F32))
        sm = Sems(nc, es, ["ld", "wld", "wla", "wlb", "pe", "act", "dve", "st"])
        block = es.enter_context(nc.Block())

        xc = lambda c: x_sb[:, c * TPC:(c + 1) * TPC]
        boc = lambda c: bo_sb[:, c * TPC:(c + 1) * TPC]
        atc = lambda c: at_sb[:, c * TPC:(c + 1) * TPC]
        ps_st = lambda s: ps[:, s * 512:(s + 1) * 512]
        ps_bc = ps[:, 4 * 512:6 * 512]

        @block.sync
        def _(sync):
            sync.dma_start(out=x_sb.ap().rearrange("p (c t) -> p c t", t=TPC),
                           in_=x_d.rearrange("(c p) t -> p c t", p=128)).then_inc(sm.inc("ld", 16), 16)

        @block.gpsimd
        def _(gpsimd):
            gpsimd.dma_start(out=at_sb.ap().rearrange("p (c t) -> p c t", t=TPC),
                             in_=at_d.rearrange("(c p) t -> p c t", p=128)).then_inc(sm.inc("wld", 16), 16)

        def evac_bo(block, oc0, ow, s, pst):
            @block.scalar
            def _(scalar):
                scalar.wait_ge(sm["pe"], sm.n["pe"])
                ins = scalar.activation(
                    bo_sb[0:ow, (oc0 // 128) * TPC + s * 512:(oc0 // 128) * TPC + s * 512 + 512],
                    pst[0:ow, :], AF.Copy)
                ins.then_inc(sm.inc("act"), 1)

        ld_target = sm.n["ld"]
        wld_target = sm.n["wld"]

        def _wx(eng):
            eng.wait_ge(sm["ld"], ld_target)
            eng.wait_ge(sm["wld"], wld_target)
        emit_dense(block, sm, xop=atc, kdim=D, T_=TPC, wout=D, wdram=wo_d,
                   wbuf=wbuf, ps_tile=lambda i: ps[:, i * 512:(i + 1) * 512],
                   evac=evac_bo, evac_sem="act", wait_x=_wx)

        @block.vector
        def _(vector):
            vector.memset(ones_col[:], 1.0)
            ins = vector.memset(ones1[:], 1.0)
            ins.then_inc(sm.inc("dve"), 1)

        nact_proj = sm.n["act"]
        ndve0 = sm.n["dve"]

        def wait_proj(eng):
            eng.wait_ge(sm["act"], nact_proj)
            eng.wait_ge(sm["dve"], ndve0)

        emit_ln_stats(block, sm, ps_st=ps_st, ones_col=ones_col,
                      src_stripe=lambda c, s: boc(c)[:, s * 512:(s + 1) * 512],
                      sq_buf=sq, nstripe=NS, wait=wait_proj)
        emit_ln_rows(block, sm, ps_st=ps_st, a_row=a_row, b_row=b_row,
                     srow=srow, nstripe=NS)
        emit_ln_apply(block, sm, v_chunk=boc, a_row=a_row, b_row=b_row,
                      ones1=ones1, ps_bc=ps_bc, out_chunk=xc, x_chunk=xc,
                      tmp=tmp, nstripe=NS)

        @block.sync
        def _(sync):
            sync.wait_ge(sm["dve"], sm.n["dve"])
            sync.dma_start(out=xn_d.rearrange("(c p) t -> p c t", p=128),
                           in_=x_sb.ap().rearrange("p (c t) -> p c t", t=TPC)).then_inc(sm.inc("st", 16), 16)

        ndve_now = sm.n["dve"]
        emit_ln_stats(block, sm, ps_st=ps_st, ones_col=ones_col,
                      src_stripe=lambda c, s: xc(c)[:, s * 512:(s + 1) * 512],
                      sq_buf=sq, nstripe=NS,
                      wait=lambda eng: eng.wait_ge(sm["dve"], ndve_now))
        emit_ln_rows(block, sm, ps_st=ps_st, a_row=a_row, b_row=b_row,
                     srow=srow, nstripe=NS)
        emit_ln_apply(block, sm, v_chunk=xc, a_row=a_row, b_row=b_row,
                      ones1=ones1, ps_bc=ps_bc, out_chunk=boc, x_chunk=None,
                      tmp=tmp, nstripe=NS)

        @block.sync
        def _(sync):
            sync.wait_ge(sm["dve"], sm.n["dve"])
            sync.dma_start(out=hn_d.rearrange("(c p) t -> p c t", p=128),
                           in_=bo_sb.ap().rearrange("p (c t) -> p c t", t=TPC)).then_inc(sm.inc("st", 16), 16)
            sync.wait_ge(sm["st"], sm.n["st"])
    return nc


def projglue_host(x_full, attn_full, wo):
    ins = []
    for c in range(NCORE):
        b, p = c // 2, c % 2
        sl = slice(p * TPC, (p + 1) * TPC)
        ins.append({"x": _fm(x_full[b, sl]), "attn": _fm(attn_full[b, sl]),
                    "wo": np.ascontiguousarray(wo, np.float32)})
    res = _run("projglue", build_projglue, ins)
    xn = np.empty((B, S, D), np.float32)
    hn = np.empty((B, S, D), np.float32)
    for c in range(NCORE):
        b, p = c // 2, c % 2
        sl = slice(p * TPC, (p + 1) * TPC)
        xn[b, sl] = res.results[c]["xn"].T
        hn[b, sl] = res.results[c]["hn"].T
    return xn, hn



def build_l1ffn():
    DH = 4096
    nc = bass.Bass("TRN2", target_bir_lowering=False)
    h_d = nc.dram_tensor("h", [D, TPC], F32, kind="ExternalInput").ap()
    w1_d = nc.dram_tensor("w1t", [D, DH], F32, kind="ExternalInput").ap()
    w2_d = nc.dram_tensor("w2t", [D, DH], F32, kind="ExternalInput").ap()
    h1_d = nc.dram_tensor("h1t", [DH, DH], F32, kind="ExternalInput").ap()
    h2_d = nc.dram_tensor("h2t", [DH, DH], F32, kind="ExternalInput").ap()
    w3_d = nc.dram_tensor("w3t", [DH, D], F32, kind="ExternalInput").ap()
    sc_d = nc.dram_tensor("scales", [128, 4 * 32 + 8], F32, kind="ExternalInput").ap()
    ffn_d = nc.dram_tensor("ffn", [D, TPC], F32, kind="ExternalOutput").ap()
    t1_d = nc.dram_tensor("t1b", [DH, TPC], BF16, kind="Internal").ap()
    hb_d = nc.dram_tensor("hbb", [DH, TPC], BF16, kind="Internal").ap()
    NS = TPC // 512
    BW = 256
    es = ExitStack()
    with es:
        xin = es.enter_context(nc.sbuf_tensor("xin", [128, 32 * TPC], BF16))
        h_sb = es.enter_context(nc.sbuf_tensor("h_sb", [128, NCH * TPC], BF16))
        wbig = es.enter_context(nc.sbuf_tensor("wbig", [128, 2 * 32 * BW], BF16))
        stg = es.enter_context(nc.sbuf_tensor("stg", [128, 4 * 512], BF16))
        stgf = es.enter_context(nc.sbuf_tensor("stgf", [128, 4 * 512], F32))
        sc_sb = es.enter_context(nc.sbuf_tensor("sc_sb", [128, 4 * 32 + 8], F32))
        ps = es.enter_context(nc.psum_tensor("ps", [128, 8 * 512], F32))
        sm = Sems(nc, es, ["ld", "hld", "wla", "wlb", "pe", "act", "dve", "st",
                           "st0", "st1", "st2", "st3"])
        block = es.enter_context(nc.Block())

        wl_name = ["wla", "wlb"]
        stn = ["st0", "st1", "st2", "st3"]
        hc = lambda c: h_sb[:, c * TPC:(c + 1) * TPC]
        xinc = lambda c: xin[:, c * TPC:(c + 1) * TPC]
        pst = lambda i: ps[:, i * 512:(i + 1) * 512]
        s1c = lambda oc: sc_sb[:, oc:oc + 1]
        s2c = lambda oc: sc_sb[:, 32 + oc:33 + oc]
        sh1c = lambda oc: sc_sb[:, 64 + oc:65 + oc]
        sh2c = lambda oc: sc_sb[:, 96 + oc:97 + oc]
        s3c = lambda oc: sc_sb[:, 128 + oc:129 + oc]

        @block.gpsimd
        def _(gpsimd):
            gpsimd.dma_start(out=h_sb.ap().rearrange("p (c t) -> p c t", t=TPC),
                             in_=h_d.rearrange("(c p) t -> p c t", p=128)).then_inc(sm.inc("hld", 16), 16)

        @block.sync
        def _(sync):
            sync.dma_start(out=sc_sb[:], in_=sc_d[:]).then_inc(sm.inc("ld", 16), 16)

        ld0 = sm.n["ld"]

        nk1 = NCH
        w1b = lambda pb: wbig[:, pb * nk1 * BW:(pb + 1) * nk1 * BW]
        w2b = lambda pb: wbig[:, (2 + pb) * nk1 * BW:(3 + pb) * nk1 * BW]
        pe_after = []
        ti_c = [0]
        st_base = sm.n["st"]
        st_tgt1 = {}
        dve_b1 = sm.n["dve"]
        wl_tgt1 = {}
        for ob in range(DH // BW):
            pb = ob % 2

            @block.gpsimd
            def _(gpsimd, ob=ob, pb=pb):
                if ob >= 2:
                    gpsimd.wait_ge(sm["pe"], pe_after[ob - 2])
                for kc in range(nk1):
                    gpsimd.dma_start(
                        out=w1b(pb)[:, kc * BW:(kc + 1) * BW],
                        in_=w1_d[kc * 128:(kc + 1) * 128, ob * BW:(ob + 1) * BW]).then_inc(sm.inc(wl_name[pb], 16), 16)
                    gpsimd.dma_start(
                        out=w2b(pb)[:, kc * BW:(kc + 1) * BW],
                        in_=w2_d[kc * 128:(kc + 1) * 128, ob * BW:(ob + 1) * BW]).then_inc(sm.inc(wl_name[pb], 16), 16)
            wl_tgt1[ob] = sm.n[wl_name[pb]]

            for oc in range(BW // 128):
                for s in range(NS):
                    ti = ti_c[0]
                    ti_c[0] += 1
                    og = ob * (BW // 128) + oc

                    @block.tensor
                    def _(tensor, ob=ob, pb=pb, oc=oc, s=s, ti=ti):
                        if ti == 0:
                            tensor.wait_ge(sm["ld"], ld0)
                            tensor.wait_ge(sm["hld"], 16)
                        tensor.wait_ge(sm[wl_name[pb]], wl_tgt1[ob])
                        if ti >= 2:
                            tensor.wait_ge(sm["dve"], dve_b1 + ti - 1)
                        for kc in range(nk1):
                            tensor.matmul(pst(2 * (ti % 2))[:],
                                          w1b(pb)[:, kc * BW + oc * 128:kc * BW + oc * 128 + 128],
                                          hc(kc)[:, s * 512:(s + 1) * 512],
                                          start=(kc == 0), stop=(kc == nk1 - 1))
                        for kc in range(nk1):
                            ins = tensor.matmul(pst(2 * (ti % 2) + 1)[:],
                                                w2b(pb)[:, kc * BW + oc * 128:kc * BW + oc * 128 + 128],
                                                hc(kc)[:, s * 512:(s + 1) * 512],
                                                start=(kc == 0), stop=(kc == nk1 - 1))
                        ins.then_inc(sm.inc("pe"), 1)

                    @block.scalar
                    def _(scalar, ti=ti, og=og):
                        scalar.wait_ge(sm["pe"], sm.n["pe"])
                        if ti >= 4:
                            scalar.wait_ge(sm[stn[ti % 4]], st_tgt1[ti - 4])
                        ins = scalar.activation(stg[:, (ti % 4) * 512:(ti % 4 + 1) * 512],
                                                pst(2 * (ti % 2))[:], AF.Silu,
                                                scale=s1c(og))
                        ins.then_inc(sm.inc("act"), 1)

                    @block.vector
                    def _(vector, ti=ti, og=og):
                        vector.wait_ge(sm["act"], sm.n["act"])
                        ins = vector.scalar_tensor_tensor(
                            stg[:, (ti % 4) * 512:(ti % 4 + 1) * 512],
                            pst(2 * (ti % 2) + 1)[:], s2c(og),
                            stg[:, (ti % 4) * 512:(ti % 4 + 1) * 512],
                            OP.mult, OP.mult)
                        ins.then_inc(sm.inc("dve"), 1)

                    @block.sync
                    def _(sync, ti=ti, og=og, s=s):
                        sync.wait_ge(sm["dve"], sm.n["dve"])
                        sync.dma_start(
                            out=t1_d[og * 128:(og + 1) * 128, s * 512:(s + 1) * 512],
                            in_=stg[:, (ti % 4) * 512:(ti % 4 + 1) * 512]).then_inc(sm.inc(stn[ti % 4], 16), 16)
                    st_tgt1[ti] = sm.n[stn[ti % 4]]
            pe_after.append(sm.n["pe"])

        @block.sync
        def _(sync):
            for _sn in stn:
                sync.wait_ge(sm[_sn], sm.n[_sn])
            sync.dma_start(out=xin.ap().rearrange("p (c t) -> p c t", t=TPC),
                           in_=t1_d.rearrange("(c p) t -> p c t", p=128)).then_inc(sm.inc("ld", 16), 16)

        def mid_stage(wsrc, dst_d, scol, in_chunk, nkin):
            ld_t = sm.n["ld"]
            pe_after2 = []
            ti2 = [0]
            st_b = sm.n["st"]
            act_b = sm.n["act"]
            pe_t0 = sm.n["pe"]
            wb = lambda pb: wbig[:, pb * nkin * BW:(pb + 1) * nkin * BW]
            wout_ = wsrc.shape[1]
            wl_tgt2 = {}
            st_tgt2 = {}
            for ob in range(wout_ // BW):
                pb = ob % 2

                @block.gpsimd
                def _(gpsimd, ob=ob, pb=pb):
                    gpsimd.wait_ge(sm["ld"], ld_t)
                    if ob >= 2:
                        gpsimd.wait_ge(sm["pe"], pe_after2[ob - 2])
                    else:
                        gpsimd.wait_ge(sm["pe"], pe_t0)
                    for kc in range(nkin):
                        gpsimd.dma_start(
                            out=wb(pb)[:, kc * BW:(kc + 1) * BW],
                            in_=wsrc[kc * 128:(kc + 1) * 128, ob * BW:(ob + 1) * BW]).then_inc(sm.inc(wl_name[pb], 16), 16)
                wl_tgt2[ob] = sm.n[wl_name[pb]]

                for oc in range(BW // 128):
                    for s in range(NS):
                        ti = ti2[0]
                        ti2[0] += 1
                        og = ob * (BW // 128) + oc

                        @block.tensor
                        def _(tensor, ob=ob, pb=pb, oc=oc, s=s, ti=ti):
                            if ti == 0:
                                tensor.wait_ge(sm["ld"], sm.n["ld"])
                            tensor.wait_ge(sm[wl_name[pb]], wl_tgt2[ob])
                            if ti >= 4:
                                tensor.wait_ge(sm["act"], act_b + ti - 3)
                            ins = None
                            for kc in range(nkin):
                                ins = tensor.matmul(pst(4 + ti % 4)[:],
                                                    wb(pb)[:, kc * BW + oc * 128:kc * BW + oc * 128 + 128],
                                                    in_chunk(kc)[:, s * 512:(s + 1) * 512],
                                                    start=(kc == 0), stop=(kc == nkin - 1))
                            ins.then_inc(sm.inc("pe"), 1)

                        @block.scalar
                        def _(scalar, ti=ti, og=og, s=s):
                            scalar.wait_ge(sm["pe"], sm.n["pe"])
                            if ti >= 4:
                                scalar.wait_ge(sm[stn[ti % 4]], st_tgt2[ti - 4])
                            ins = scalar.activation(stg[:, (ti % 4) * 512:(ti % 4 + 1) * 512],
                                                    pst(4 + ti % 4)[:], AF.Gelu,
                                                    scale=scol(og))
                            ins.then_inc(sm.inc("act"), 1)

                        @block.sync
                        def _(sync, ti=ti, og=og, s=s):
                            sync.wait_ge(sm["act"], sm.n["act"])
                            sync.dma_start(
                                out=dst_d[og * 128:(og + 1) * 128, s * 512:(s + 1) * 512],
                                in_=stg[:, (ti % 4) * 512:(ti % 4 + 1) * 512]).then_inc(sm.inc(stn[ti % 4], 16), 16)
                        st_tgt2[ti] = sm.n[stn[ti % 4]]
                pe_after2.append(sm.n["pe"])
            return

        mid_stage(h1_d, hb_d, sh1c, xinc, 32)

        @block.sync
        def _(sync):
            for _sn in stn:
                sync.wait_ge(sm[_sn], sm.n[_sn])
            sync.dma_start(out=xin.ap().rearrange("p (c t) -> p c t", t=TPC),
                           in_=hb_d.rearrange("(c p) t -> p c t", p=128)).then_inc(sm.inc("ld", 16), 16)

        mid_stage(h2_d, t1_d, sh2c, xinc, 32)

        @block.sync
        def _(sync):
            for _sn in stn:
                sync.wait_ge(sm[_sn], sm.n[_sn])
            sync.dma_start(out=xin.ap().rearrange("p (c t) -> p c t", t=TPC),
                           in_=t1_d.rearrange("(c p) t -> p c t", p=128)).then_inc(sm.inc("ld", 16), 16)

        ld_t4 = sm.n["ld"]
        pe_after4 = []
        ti4 = [0]
        st_b4 = sm.n["st"]
        act_b4 = sm.n["act"]
        pe_t4 = sm.n["pe"]
        st_tgt4 = {}
        nk4 = 32
        w4b = lambda pb: wbig[:, pb * nk4 * BW:(pb + 1) * nk4 * BW]
        wl_tgt4 = {}
        for ob in range(D // BW):
            pb = ob % 2

            @block.gpsimd
            def _(gpsimd, ob=ob, pb=pb):
                gpsimd.wait_ge(sm["ld"], ld_t4)
                if ob >= 2:
                    gpsimd.wait_ge(sm["pe"], pe_after4[ob - 2])
                else:
                    gpsimd.wait_ge(sm["pe"], pe_t4)
                for kc in range(nk4):
                    gpsimd.dma_start(
                        out=w4b(pb)[:, kc * BW:(kc + 1) * BW],
                        in_=w3_d[kc * 128:(kc + 1) * 128, ob * BW:(ob + 1) * BW]).then_inc(sm.inc(wl_name[pb], 16), 16)
            wl_tgt4[ob] = sm.n[wl_name[pb]]

            for oc in range(BW // 128):
                for s in range(NS):
                    ti = ti4[0]
                    ti4[0] += 1
                    og = ob * (BW // 128) + oc

                    @block.tensor
                    def _(tensor, ob=ob, pb=pb, oc=oc, s=s, ti=ti):
                        if ti == 0:
                            tensor.wait_ge(sm["ld"], sm.n["ld"])
                        tensor.wait_ge(sm[wl_name[pb]], wl_tgt4[ob])
                        if ti >= 4:
                            tensor.wait_ge(sm["act"], act_b4 + ti - 3)
                        ins = None
                        for kc in range(nk4):
                            ins = tensor.matmul(pst(4 + ti % 4)[:],
                                                w4b(pb)[:, kc * BW + oc * 128:kc * BW + oc * 128 + 128],
                                                xinc(kc)[:, s * 512:(s + 1) * 512],
                                                start=(kc == 0), stop=(kc == nk4 - 1))
                        ins.then_inc(sm.inc("pe"), 1)

                    @block.scalar
                    def _(scalar, ti=ti, og=og, s=s):
                        scalar.wait_ge(sm["pe"], sm.n["pe"])
                        if ti >= 4:
                            scalar.wait_ge(sm[stn[ti % 4]], st_tgt4[ti - 4])
                        ins = scalar.activation(stgf[:, (ti % 4) * 512:(ti % 4 + 1) * 512],
                                                pst(4 + ti % 4)[:], AF.Copy,
                                                scale=s3c(og))
                        ins.then_inc(sm.inc("act"), 1)

                    @block.sync
                    def _(sync, ti=ti, og=og, s=s):
                        sync.wait_ge(sm["act"], sm.n["act"])
                        sync.dma_start(
                            out=ffn_d[og * 128:(og + 1) * 128, s * 512:(s + 1) * 512],
                            in_=stgf[:, (ti % 4) * 512:(ti % 4 + 1) * 512]).then_inc(sm.inc(stn[ti % 4], 16), 16)
                    st_tgt4[ti] = sm.n[stn[ti % 4]]
            pe_after4.append(sm.n["pe"])

        @block.sync
        def _(sync):
            sync.wait_ge(sm["st"], sm.n["st"])
            for _sn in stn:
                sync.wait_ge(sm[_sn], sm.n[_sn])
    return nc


def l1ffn_host(h_full, inputs):
    w1t = np.ascontiguousarray(inputs["L1_ffn_w1_w"].T, np.float32)
    w2t = np.ascontiguousarray(inputs["L1_ffn_w2_w"].T, np.float32)
    h1t = np.ascontiguousarray(inputs["L1_ffn_h1_w"].T, np.float32)
    h2t = np.ascontiguousarray(inputs["L1_ffn_h2_w"].T, np.float32)
    w3t = np.ascontiguousarray(inputs["L1_ffn_w3_w"].T, np.float32)
    sc = np.zeros((128, 4 * 32 + 8), np.float32)
    for i, k in enumerate(["L1_ffn_w1_s", "L1_ffn_w2_s", "L1_ffn_h1_s", "L1_ffn_h2_s"]):
        sc[:, i * 32:(i + 1) * 32] = inputs[k].reshape(32, 128).T
    sc[:, 128:136] = inputs["L1_ffn_w3_s"].reshape(8, 128).T
    ins = []
    for c in range(NCORE):
        b, p = c // 2, c % 2
        sl = slice(p * TPC, (p + 1) * TPC)
        ins.append({"h": _fm(h_full[b, sl]), "w1t": w1t, "w2t": w2t,
                    "h1t": h1t, "h2t": h2t, "w3t": w3t, "scales": sc})
    res = _run("l1ffn", build_l1ffn, ins)
    ffn = np.empty((B, S, D), np.float32)
    for c in range(NCORE):
        b, p = c // 2, c % 2
        ffn[c // 2, slice(p * TPC, (p + 1) * TPC)] = res.results[c]["ffn"].T
    return ffn



def build_attn(kind):
    causal = kind == "l0"
    NG = 2 if causal else 8
    REP = 4 if causal else 1
    KVW = 128 if causal else 512
    NQ = S
    NST = NQ // 512
    NKC = NQ // 128
    SCALE = 0.125
    nc = bass.Bass("TRN2", target_bir_lowering=False)
    x_d = nc.dram_tensor("x", [D, NQ], F32, kind="ExternalInput").ap()
    wq_d = nc.dram_tensor("wq", [D, 512], F32, kind="ExternalInput").ap()
    wk_d = nc.dram_tensor("wk", [D, KVW], F32, kind="ExternalInput").ap()
    wv_d = nc.dram_tensor("wv", [D, KVW], F32, kind="ExternalInput").ap()
    at_d = nc.dram_tensor("attn", [512, NQ], BF16, kind="ExternalOutput").ap()

    es = ExitStack()
    with es:
        hn = es.enter_context(nc.sbuf_tensor("hn", [128, NCH * NQ], BF16))
        qa = es.enter_context(nc.sbuf_tensor("qa", [65, NQ], BF16))
        ka = es.enter_context(nc.sbuf_tensor("ka", [65, NQ], BF16))
        vt = es.enter_context(nc.sbuf_tensor("vt", [128, NKC * 65], BF16))
        wq_sb = es.enter_context(nc.sbuf_tensor("wq_sb", [128, NCH * 512], BF16))
        wk_sb = es.enter_context(nc.sbuf_tensor("wk_sb", [128, NCH * KVW], BF16))
        wv_sb = es.enter_context(nc.sbuf_tensor("wv_sb", [128, NCH * KVW], BF16))
        o_sb = es.enter_context(nc.sbuf_tensor("o_sb", [128, 4 * NQ], BF16))
        psb = es.enter_context(nc.sbuf_tensor("psb", [128, 4 * 512], BF16))
        rm = es.enter_context(nc.sbuf_tensor("rm", [128, 2], F32))
        nm32 = es.enter_context(nc.sbuf_tensor("nm32", [128, 1], F32))
        ident = es.enter_context(nc.sbuf_tensor("ident", [128, 128], F32))
        rec = es.enter_context(nc.sbuf_tensor("rec", [1, 512], F32))
        rb_sb = es.enter_context(nc.sbuf_tensor("rb_sb", [64, 512], F32))
        ones1 = es.enter_context(nc.sbuf_tensor("ones1", [1, 128], F32))
        if causal:
            xrot = es.enter_context(nc.sbuf_tensor("xrot", [128, NQ], BF16))
            sq = es.enter_context(nc.sbuf_tensor("sq", [128, 2 * 512], BF16))
            bc_a = es.enter_context(nc.sbuf_tensor("bc_a", [128, NST * 512], F32))
            bc_b = es.enter_context(nc.sbuf_tensor("bc_b", [128, NST * 512], F32))
            a_row = es.enter_context(nc.sbuf_tensor("a_row", [1, 512], F32))
            b_row = es.enter_context(nc.sbuf_tensor("b_row", [1, 512], F32))
            srow = es.enter_context(nc.sbuf_tensor("srow", [1, 3 * 512], F32))
            ones_col = es.enter_context(nc.sbuf_tensor("ones_col", [128, 1], BF16))
            tmp = es.enter_context(nc.sbuf_tensor("tmp", [128, 512], F32))
            msk = es.enter_context(nc.sbuf_tensor("msk", [128, 4 * 512], F32))
            mskt = es.enter_context(nc.sbuf_tensor("mskt", [128, 4 * 512], F32))
        ps = es.enter_context(nc.psum_tensor("ps", [128, 8 * 512], F32))
        sm = Sems(nc, es, ["ld", "wld", "gld", "pe", "act", "dve", "st"])
        block = es.enter_context(nc.Block())

        hc = lambda c: hn[:, c * NQ:(c + 1) * NQ]
        pbank = lambda i: ps[:, i * 512:(i + 1) * 512]

        @block.gpsimd
        def _(gpsimd):
            gpsimd.dma_start(out=wq_sb.ap().rearrange("p (c w) -> p c w", w=512),
                             in_=wq_d.rearrange("(c p) w -> p c w", p=128)).then_inc(sm.inc("wld", 16), 16)
            gpsimd.dma_start(out=wk_sb.ap().rearrange("p (c w) -> p c w", w=KVW),
                             in_=wk_d.rearrange("(c p) w -> p c w", p=128)).then_inc(sm.inc("wld", 16), 16)
            gpsimd.dma_start(out=wv_sb.ap().rearrange("p (c w) -> p c w", w=KVW),
                             in_=wv_d.rearrange("(c p) w -> p c w", p=128)).then_inc(sm.inc("wld", 16), 16)

        identd = nc.inline_tensor(np.eye(128, dtype=np.float32), "identd").ap()

        @block.sync
        def _(sync):
            sync.dma_start(out=ident[:], in_=identd[:]).then_inc(sm.inc("ld", 16), 16)

        @block.vector
        def _(vector):
            vector.memset(ones1[:], 1.0)
            vector.memset(ka[64:65, :], 1.0)
            ins = vector.memset(vt[:, 64::65], 1.0)
            if causal:
                ins = vector.memset(ones_col[:], 1.0)
            ins.then_inc(sm.inc("dve"), 1)

        if causal:
            msk_np = np.zeros((128, 4 * 512), np.float32)
            mskt_np = np.zeros((128, 4 * 512), np.float32)
            ii = np.arange(128)[:, None]
            jj = np.arange(512)[None, :]
            for v in range(4):
                msk_np[:, v * 512:(v + 1) * 512] = np.where(jj <= v * 128 + ii, 0.0, NEG)
                mskt_np[:, v * 512:(v + 1) * 512] = np.where(jj >= v * 128 + ii, 0.0, NEG)
            mskd = nc.inline_tensor(msk_np, "mskd").ap()
            msktd = nc.inline_tensor(mskt_np, "msktd").ap()

            @block.sync
            def _(sync):
                sync.dma_start(out=msk[:], in_=mskd[:]).then_inc(sm.inc("ld", 16), 16)
                sync.dma_start(out=mskt[:], in_=msktd[:]).then_inc(sm.inc("ld", 16), 16)

            pe_b = sm.n["pe"]
            for c in range(NCH):
                @block.gpsimd
                def _(gpsimd, c=c):
                    if c > 0:
                        gpsimd.wait_ge(sm["pe"], pe_b + c * NST)
                    gpsimd.dma_start(out=xrot[:],
                                     in_=x_d[c * 128:(c + 1) * 128, :]).then_inc(sm.inc("gld", 16), 16)

                for s in range(NST):
                    g = c * NST + s

                    @block.scalar
                    def _(scalar, c=c, s=s, g=g):
                        scalar.wait_ge(sm["ld"], sm.n["ld"])
                        scalar.wait_ge(sm["gld"], sm.n["gld"])
                        if g >= 2:
                            scalar.wait_ge(sm["pe"], pe_b + g - 1)
                        ins = scalar.activation(sq[:, (g % 2) * 512:(g % 2 + 1) * 512],
                                                xrot[:, s * 512:(s + 1) * 512], AF.Square)
                        ins.then_inc(sm.inc("act"), 1)

                    @block.tensor
                    def _(tensor, c=c, s=s, g=g):
                        tensor.wait_ge(sm["act"], sm.n["act"])
                        tensor.matmul(pbank(s)[0:1, :], ones_col[:, 0:1],
                                      xrot[:, s * 512:(s + 1) * 512],
                                      start=(c == 0), stop=(c == NCH - 1))
                        ins = tensor.matmul(pbank(s)[32:33, :], ones_col[:, 0:1],
                                            sq[:, (g % 2) * 512:(g % 2 + 1) * 512],
                                            start=(c == 0), stop=(c == NCH - 1))
                        ins.then_inc(sm.inc("pe"), 1)

            for s in range(NST):
                emit_ln_rows(block, sm, ps_st=lambda _s, s=s: pbank(s),
                             a_row=a_row, b_row=b_row, srow=srow, nstripe=1)

                @block.tensor
                def _(tensor, s=s):
                    tensor.wait_ge(sm["dve"], sm.n["dve"])
                    ins = tensor.matmul(pbank(s)[:], ones1[0:1, :], a_row[0:1, :],
                                        start=True, stop=True)
                    ins.then_inc(sm.inc("pe"), 1)

                @block.vector
                def _(vector, s=s):
                    vector.wait_ge(sm["pe"], sm.n["pe"])
                    ins = vector.tensor_copy(bc_a[:, s * 512:(s + 1) * 512], pbank(s)[:])
                    ins.then_inc(sm.inc("dve"), 1)

                @block.tensor
                def _(tensor, s=s):
                    tensor.wait_ge(sm["dve"], sm.n["dve"])
                    ins = tensor.matmul(pbank(s)[:], ones1[0:1, :], b_row[0:1, :],
                                        start=True, stop=True)
                    ins.then_inc(sm.inc("pe"), 1)

                @block.vector
                def _(vector, s=s):
                    vector.wait_ge(sm["pe"], sm.n["pe"])
                    ins = vector.tensor_copy(bc_b[:, s * 512:(s + 1) * 512], pbank(s)[:])
                    ins.then_inc(sm.inc("dve"), 1)

            dve_apply0 = sm.n["dve"]
            for c in range(NCH):
                @block.gpsimd
                def _(gpsimd, c=c):
                    if c > 0:
                        gpsimd.wait_ge(sm["dve"], dve_apply0 + c * NST)
                    else:
                        gpsimd.wait_ge(sm["dve"], dve_apply0)
                    gpsimd.dma_start(out=xrot[:],
                                     in_=x_d[c * 128:(c + 1) * 128, :]).then_inc(sm.inc("gld", 16), 16)

                for s in range(NST):
                    @block.vector
                    def _(vector, c=c, s=s):
                        vector.wait_ge(sm["gld"], sm.n["gld"])
                        vector.tensor_add(tmp[:], xrot[:, s * 512:(s + 1) * 512],
                                          bc_b[:, s * 512:(s + 1) * 512])
                        ins = vector.tensor_mul(hc(c)[:, s * 512:(s + 1) * 512],
                                                tmp[:], bc_a[:, s * 512:(s + 1) * 512])
                        ins.then_inc(sm.inc("dve"), 1)
        else:
            @block.gpsimd
            def _(gpsimd):
                gpsimd.dma_start(out=hn.ap().rearrange("p (c t) -> p c t", t=NQ),
                                 in_=x_d.rearrange("(c p) t -> p c t", p=128)).then_inc(sm.inc("gld", 16), 16)

        hn_dve = sm.n["dve"]
        hn_ld = sm.n["ld"]
        hn_gld = sm.n["gld"]
        wld_t = sm.n["wld"]

        for g in range(NG):
            gcol = g * 64
            act_kb = sm.n["act"]
            for st in range(NST):
                @block.tensor
                def _(tensor, st=st, actn=sm.n["act"]):
                    if st == 0:
                        tensor.wait_ge(sm["wld"], wld_t)
                        tensor.wait_ge(sm["ld"], hn_ld)
                        tensor.wait_ge(sm["gld"], hn_gld)
                        tensor.wait_ge(sm["dve"], sm.n["dve"])
                    if st < 2:
                        tensor.wait_ge(sm["act"], actn)
                    else:
                        tensor.wait_ge(sm["act"], act_kb + st - 1)
                    for kc in range(NCH):
                        ins = tensor.matmul(pbank(6 + st % 2)[0:64, :],
                                            wk_sb[:, kc * KVW + gcol:kc * KVW + gcol + 64],
                                            hc(kc)[:, st * 512:(st + 1) * 512],
                                            start=(kc == 0), stop=(kc == NCH - 1))
                    ins.then_inc(sm.inc("pe"), 1)

                @block.scalar
                def _(scalar, st=st):
                    scalar.wait_ge(sm["pe"], sm.n["pe"])
                    ins = scalar.activation(ka[0:64, st * 512:(st + 1) * 512],
                                            pbank(6 + st % 2)[0:64, :], AF.Copy)
                    ins.then_inc(sm.inc("act"), 1)

            dve_vb = sm.n["dve"]
            for tc in range(NKC):
                @block.tensor
                def _(tensor, tc=tc, actn=sm.n["act"]):
                    if tc < 2:
                        tensor.wait_ge(sm["act"], actn)
                    else:
                        tensor.wait_ge(sm["dve"], dve_vb + tc - 1)
                    for kc in range(NCH):
                        ins = tensor.matmul(pbank(6 + tc % 2)[:, 0:64],
                                            hc(kc)[:, tc * 128:(tc + 1) * 128],
                                            wv_sb[:, kc * KVW + gcol:kc * KVW + gcol + 64],
                                            start=(kc == 0), stop=(kc == NCH - 1))
                    ins.then_inc(sm.inc("pe"), 1)

                @block.vector
                def _(vector, tc=tc):
                    vector.wait_ge(sm["pe"], sm.n["pe"])
                    ins = vector.tensor_copy(vt[:, tc * 65:tc * 65 + 64],
                                             pbank(6 + tc % 2)[:, 0:64])
                    ins.then_inc(sm.inc("dve"), 1)

            for r in range(REP):
                hq = g * REP + r
                qcol = hq * 64
                och = hq // 2
                orow = (hq % 2) * 64

                act_qb = sm.n["act"]
                for st in range(NST):
                    @block.tensor
                    def _(tensor, st=st, dven=sm.n["dve"]):
                        if st < 2:
                            tensor.wait_ge(sm["dve"], dven)
                        else:
                            tensor.wait_ge(sm["act"], act_qb + st - 1)
                        for kc in range(NCH):
                            ins = tensor.matmul(pbank(6 + st % 2)[0:64, :],
                                                wq_sb[:, kc * 512 + qcol:kc * 512 + qcol + 64],
                                                hc(kc)[:, st * 512:(st + 1) * 512],
                                                start=(kc == 0), stop=(kc == NCH - 1))
                        ins.then_inc(sm.inc("pe"), 1)

                    @block.scalar
                    def _(scalar, st=st):
                        scalar.wait_ge(sm["pe"], sm.n["pe"])
                        ins = scalar.activation(qa[0:64, st * 512:(st + 1) * 512],
                                                pbank(6 + st % 2)[0:64, :], AF.Copy,
                                                scale=SCALE)
                        ins.then_inc(sm.inc("act"), 1)

                qproj_act = sm.n["act"]

                for st in range(NST):
                    for sub in range(4):
                        qt = st * 4 + sub
                        n512 = (qt // 4 + 1) if causal else NST
                        dve_sb = sm.n["dve"]
                        for j in range(n512):
                            @block.tensor
                            def _(tensor, qt=qt, j=j, i2=j, dve_sb=dve_sb):
                                tensor.wait_ge(sm["act"], qproj_act)
                                if i2 < 2:
                                    tensor.wait_ge(sm["dve"], dve_sb)
                                else:
                                    tensor.wait_ge(sm["dve"], dve_sb + i2 - 1)
                                ins = tensor.matmul(pbank(i2 % 2)[:],
                                                    qa[0:64, qt * 128:(qt + 1) * 128],
                                                    ka[0:64, j * 512:(j + 1) * 512],
                                                    start=True, stop=True)
                                ins.then_inc(sm.inc("pe"), 1)

                            @block.vector
                            def _(vector, qt=qt, j=j, i2=j, n512=n512):
                                vector.wait_ge(sm["pe"], sm.n["pe"])
                                bank = pbank(i2 % 2)
                                if causal and j == qt // 4:
                                    vector.tensor_add(bank[:], bank[:],
                                                      msk[:, (qt % 4) * 512:(qt % 4 + 1) * 512])
                                if j == 0:
                                    ins = vector.tensor_reduce(rm[:, 0:1], bank[:],
                                                               axis=AX.X, op=OP.max)
                                else:
                                    vector.tensor_reduce(rm[:, 1:2], bank[:],
                                                         axis=AX.X, op=OP.max)
                                    ins = vector.tensor_max(rm[:, 0:1], rm[:, 0:1],
                                                            rm[:, 1:2])
                                ins.then_inc(sm.inc("dve"), 1)

                        @block.vector
                        def _(vector, qt=qt):
                            ins = vector.tensor_scalar_mul(nm32[:, 0:1], rm[:, 0:1], -1.0)
                            ins.then_inc(sm.inc("dve"), 1)

                        @block.tensor
                        def _(tensor, qt=qt):
                            tensor.wait_ge(sm["dve"], sm.n["dve"])
                            ins = tensor.matmul(pbank(5)[0:1, 0:128],
                                                nm32[:, 0:1], ident[:],
                                                start=True, stop=True)
                            ins.then_inc(sm.inc("pe"), 1)

                        @block.vector
                        def _(vector, qt=qt):
                            vector.wait_ge(sm["pe"], sm.n["pe"])
                            ins = vector.tensor_copy(qa[64:65, qt * 128:(qt + 1) * 128],
                                                     pbank(5)[0:1, 0:128])
                            ins.then_inc(sm.inc("dve"), 1)

                    row_dve = sm.n["dve"]
                    act_stb = sm.n["act"]

                    nk128 = 4 * (st + 1) if causal else NKC
                    st_pe = {}
                    av_pe = {}
                    exp_act = {}
                    msk_dve = {}

                    def emit_st(j2, st=st, nk128=nk128):
                        @block.tensor
                        def _(tensor, j2=j2):
                            tensor.wait_ge(sm["dve"], row_dve)
                            if j2 < 2:
                                tensor.wait_ge(sm["act"], act_stb)
                            else:
                                tensor.wait_ge(sm["act"], exp_act[j2 - 2])
                            ins = tensor.matmul(pbank(2 + j2 % 2)[:],
                                                ka[:, j2 * 128:(j2 + 1) * 128],
                                                qa[:, st * 512:(st + 1) * 512],
                                                start=True, stop=True)
                            ins.then_inc(sm.inc("pe"), 1)
                        st_pe[j2] = sm.n["pe"]
                        if causal and j2 >= nk128 - 4:
                            v = j2 - (nk128 - 4)

                            @block.vector
                            def _(vector, j2=j2, v=v):
                                vector.wait_ge(sm["pe"], st_pe[j2])
                                ins = vector.tensor_add(pbank(2 + j2 % 2)[:],
                                                        pbank(2 + j2 % 2)[:],
                                                        mskt[:, v * 512:(v + 1) * 512])
                                ins.then_inc(sm.inc("dve"), 1)
                            msk_dve[j2] = sm.n["dve"]

                        @block.scalar
                        def _(scalar, j2=j2):
                            scalar.wait_ge(sm["pe"], st_pe[j2])
                            if j2 in msk_dve:
                                scalar.wait_ge(sm["dve"], msk_dve[j2])
                            if j2 >= 4:
                                scalar.wait_ge(sm["pe"], av_pe[j2 - 4])
                            ins = scalar.activation(psb[:, (j2 % 4) * 512:(j2 % 4 + 1) * 512],
                                                    pbank(2 + j2 % 2)[:], AF.Exp)
                            ins.then_inc(sm.inc("act"), 1)
                        exp_act[j2] = sm.n["act"]

                    def emit_av(j2, st=st, nk128=nk128):
                        @block.tensor
                        def _(tensor, j2=j2):
                            tensor.wait_ge(sm["act"], exp_act[j2])
                            ins = tensor.matmul(pbank(4)[0:65, :],
                                                vt[:, j2 * 65:(j2 + 1) * 65],
                                                psb[:, (j2 % 4) * 512:(j2 % 4 + 1) * 512],
                                                start=(j2 == 0), stop=(j2 == nk128 - 1))
                            ins.then_inc(sm.inc("pe"), 1)
                        av_pe[j2] = sm.n["pe"]

                    for j2 in range(nk128):
                        emit_st(j2)
                        if j2 >= 1:
                            emit_av(j2 - 1)
                    emit_av(nk128 - 1)

                    @block.vector
                    def _(vector):
                        vector.wait_ge(sm["pe"], sm.n["pe"])
                        ins = vector.reciprocal(rec[:], pbank(4)[64:65, :])
                        ins.then_inc(sm.inc("dve"), 1)

                    @block.tensor
                    def _(tensor):
                        tensor.wait_ge(sm["dve"], sm.n["dve"])
                        ins = tensor.matmul(pbank(5)[0:64, :], ones1[0:1, 0:64],
                                            rec[0:1, :], start=True, stop=True)
                        ins.then_inc(sm.inc("pe"), 1)

                    @block.vector
                    def _(vector, st=st, och=och, orow=orow):
                        vector.wait_ge(sm["pe"], sm.n["pe"])
                        _ss(vector, sm, vector.tensor_copy(rb_sb[:],
                                                           pbank(5)[0:64, :]))
                        ins = vector.tensor_mul(
                            o_sb[orow:orow + 64, och * NQ + st * 512:och * NQ + (st + 1) * 512],
                            pbank(4)[0:64, :], rb_sb[:])
                        ins.then_inc(sm.inc("dve"), 1)

        @block.sync
        def _(sync):
            sync.wait_ge(sm["dve"], sm.n["dve"])
            sync.dma_start(out=at_d.rearrange("(c p) t -> p c t", p=128),
                           in_=o_sb.ap().rearrange("p (c t) -> p c t", t=NQ)).then_inc(sm.inc("st", 16), 16)
            sync.wait_ge(sm["st"], sm.n["st"])
    return nc


def attn_host(kind, x_full, wq, wk, wv):
    causal = kind == "l0"
    ins = []
    for c in range(NCORE):
        b, j = c // 2, c % 2
        if causal:
            wqs = wq[:, j * 512:(j + 1) * 512]
            wks = wk[:, j * 128:(j + 1) * 128]
            wvs = wv[:, j * 128:(j + 1) * 128]
        else:
            wqs = wq[:, j * 512:(j + 1) * 512]
            wks = wk[:, j * 512:(j + 1) * 512]
            wvs = wv[:, j * 512:(j + 1) * 512]
        ins.append({"x": _fm(x_full[b]), "wq": np.ascontiguousarray(wqs, np.float32),
                    "wk": np.ascontiguousarray(wks, np.float32),
                    "wv": np.ascontiguousarray(wvs, np.float32)})
    res = _run("attn_" + kind, lambda: build_attn(kind), ins)
    out = np.empty((B, S, D), np.float32)
    for c in range(NCORE):
        b, j = c // 2, c % 2
        out[b, :, j * 512:(j + 1) * 512] = res.results[c]["attn"].astype(np.float32).T
    return out



def build_c2():
    HID = 2730
    W1O = 5460
    NHC = 22
    nc = bass.Bass("TRN2", target_bir_lowering=False)
    xm_d = nc.dram_tensor("xm", [D, 1 + TPC], F32, kind="ExternalInput").ap()
    w1_d = nc.dram_tensor("w1", [D, W1O], F32, kind="ExternalInput").ap()
    w2_d = nc.dram_tensor("w2", [HID, D], F32, kind="ExternalInput").ap()
    lgb_d = nc.dram_tensor("lgb", [128, 16], F32, kind="ExternalInput").ap()
    b1_d = nc.dram_tensor("b1", [128, 2 * NHC], F32, kind="ExternalInput").ap()
    b2_d = nc.dram_tensor("b2", [128, 8], F32, kind="ExternalInput").ap()
    yn_d = nc.dram_tensor("ynext", [D, TPC], F32, kind="ExternalOutput").ap()
    TH = 1024
    NS = TH // 512
    es = ExitStack()
    with es:
        xm_sb = es.enter_context(nc.sbuf_tensor("xm_sb", [128, NCH * (1 + TPC)], F32))
        lnh = es.enter_context(nc.sbuf_tensor("lnh", [128, NCH * TH], BF16))
        hf = es.enter_context(nc.sbuf_tensor("hf", [128, NHC * TH], BF16))
        stg = es.enter_context(nc.sbuf_tensor("stg", [128, 2 * 512], BF16))
        w1b = es.enter_context(nc.sbuf_tensor("w1b", [128, 2 * NCH * 256], BF16))
        w2b = es.enter_context(nc.sbuf_tensor("w2b", [128, 2 * NHC * 256], BF16))
        sq = es.enter_context(nc.sbuf_tensor("sq", [128, 2 * 512], F32))
        lgb = es.enter_context(nc.sbuf_tensor("lgb_sb", [128, 16], F32))
        b1_sb = es.enter_context(nc.sbuf_tensor("b1_sb", [128, 2 * NHC], F32))
        b2_sb = es.enter_context(nc.sbuf_tensor("b2_sb", [128, 8], F32))
        a_row = es.enter_context(nc.sbuf_tensor("a_row", [1, 512], F32))
        b_row = es.enter_context(nc.sbuf_tensor("b_row", [1, 512], F32))
        srow = es.enter_context(nc.sbuf_tensor("srow", [1, 3 * 512], F32))
        ones_col = es.enter_context(nc.sbuf_tensor("ones_col", [128, 1], F32))
        ones1 = es.enter_context(nc.sbuf_tensor("ones1", [1, 128], F32))
        tmp = es.enter_context(nc.sbuf_tensor("tmp", [128, 512], F32))
        bc_a = es.enter_context(nc.sbuf_tensor("bc_a", [128, NS * 512], F32))
        bc_b = es.enter_context(nc.sbuf_tensor("bc_b", [128, NS * 512], F32))
        ps = es.enter_context(nc.psum_tensor("ps", [128, 8 * 512], F32))
        sm = Sems(nc, es, ["ld", "wla", "wlb", "pe", "act", "dve", "st"])
        block = es.enter_context(nc.Block())

        wl_name = ["wla", "wlb"]
        pbank = lambda i: ps[:, i * 512:(i + 1) * 512]
        xmc = lambda c: xm_sb[:, c * (1 + TPC):(c + 1) * (1 + TPC)]
        lnc = lambda c: lnh[:, c * TH:(c + 1) * TH]
        hsh = lambda c, t0, w: (xmc(c)[:, 1 + t0:1 + t0 + w] if c < 4
                                else xmc(c)[:, t0:t0 + w])

        @block.sync
        def _(sync):
            sync.dma_start(out=xm_sb.ap().rearrange("p (c t) -> p c t", t=1 + TPC),
                           in_=xm_d.rearrange("(c p) t -> p c t", p=128)).then_inc(sm.inc("ld", 16), 16)
            sync.dma_start(out=lgb[:], in_=lgb_d[:]).then_inc(sm.inc("ld", 16), 16)
            sync.dma_start(out=b1_sb[:], in_=b1_d[:]).then_inc(sm.inc("ld", 16), 16)
            sync.dma_start(out=b2_sb[:], in_=b2_d[:]).then_inc(sm.inc("ld", 16), 16)

        @block.vector
        def _(vector):
            vector.memset(ones_col[:], 1.0)
            ins = vector.memset(ones1[:], 1.0)
            ins.then_inc(sm.inc("dve"), 1)

        ld0 = sm.n["ld"]

        for th in range(2):
            t0 = th * TH
            pe_b = sm.n["pe"]
            act_b = sm.n["act"]
            for s in range(NS):
                for c in range(NCH):
                    idx = s * NCH + c

                    @block.scalar
                    def _(scalar, s=s, c=c, idx=idx):
                        if idx == 0:
                            scalar.wait_ge(sm["ld"], ld0)
                            scalar.wait_ge(sm["dve"], sm.n["dve"])
                        if idx >= 2:
                            scalar.wait_ge(sm["pe"], pe_b + idx - 1)
                        ins = scalar.activation(
                            sq[:, (idx % 2) * 512:(idx % 2 + 1) * 512],
                            hsh(c, t0 + s * 512, 512), AF.Square)
                        ins.then_inc(sm.inc("act"), 1)

                    @block.tensor
                    def _(tensor, s=s, c=c, idx=idx):
                        tensor.wait_ge(sm["act"], act_b + idx + 1)
                        tensor.matmul(pbank(s)[0:1, :], ones_col[:, 0:1],
                                      hsh(c, t0 + s * 512, 512),
                                      start=(c == 0), stop=(c == NCH - 1))
                        ins = tensor.matmul(pbank(s)[32:33, :], ones_col[:, 0:1],
                                            sq[:, (idx % 2) * 512:(idx % 2 + 1) * 512],
                                            start=(c == 0), stop=(c == NCH - 1))
                        ins.then_inc(sm.inc("pe"), 1)

            for s in range(NS):
                emit_ln_rows(block, sm, ps_st=lambda _s, s=s: pbank(s),
                             a_row=a_row, b_row=b_row, srow=srow, nstripe=1)

                @block.tensor
                def _(tensor, s=s):
                    tensor.wait_ge(sm["dve"], sm.n["dve"])
                    tensor.matmul(pbank(s)[:], ones1[0:1, :], a_row[0:1, :],
                                  start=True, stop=True).then_inc(sm.inc("pe"), 1)

                @block.vector
                def _(vector, s=s):
                    vector.wait_ge(sm["pe"], sm.n["pe"])
                    vector.tensor_copy(bc_a[:, s * 512:(s + 1) * 512],
                                       pbank(s)[:]).then_inc(sm.inc("dve"), 1)

                @block.tensor
                def _(tensor, s=s):
                    tensor.wait_ge(sm["dve"], sm.n["dve"])
                    tensor.matmul(pbank(s)[:], ones1[0:1, :], b_row[0:1, :],
                                  start=True, stop=True).then_inc(sm.inc("pe"), 1)

                @block.vector
                def _(vector, s=s):
                    vector.wait_ge(sm["pe"], sm.n["pe"])
                    vector.tensor_copy(bc_b[:, s * 512:(s + 1) * 512],
                                       pbank(s)[:]).then_inc(sm.inc("dve"), 1)

            for c in range(NCH):
                for s in range(NS):
                    @block.vector
                    def _(vector, c=c, s=s):
                        vector.wait_ge(sm["dve"], sm.n["dve"])
                        vector.tensor_add(tmp[:], hsh(c, t0 + s * 512, 512),
                                          bc_b[:, s * 512:(s + 1) * 512])
                        vector.tensor_mul(tmp[:], tmp[:],
                                          bc_a[:, s * 512:(s + 1) * 512])
                        ins = vector.tensor_scalar(lnc(c)[:, s * 512:(s + 1) * 512],
                                                   tmp[:], lgb[:, c:c + 1],
                                                   lgb[:, 8 + c:9 + c],
                                                   OP.mult, OP.add)
                        ins.then_inc(sm.inc("dve"), 1)

            lnh_dve = sm.n["dve"]
            pe_after1 = []
            ti1 = [0]
            dve_hf0 = sm.n["dve"]
            pe_w1b = sm.n["pe"]
            wl_tgt1 = {}
            for ob in range(NHC):
                ow = 128 if ob < NHC - 1 else HID - 128 * (NHC - 1)

                @block.gpsimd
                def _(gpsimd, ob=ob, ow=ow):
                    if ob >= 2:
                        gpsimd.wait_ge(sm["pe"], pe_after1[ob - 2])
                    else:
                        gpsimd.wait_ge(sm["pe"], pe_w1b)
                    pb = ob % 2
                    for kc in range(NCH):
                        gpsimd.dma_start(
                            out=w1b[0:128, (pb * NCH + kc) * 256:(pb * NCH + kc) * 256 + ow],
                            in_=w1_d[kc * 128:(kc + 1) * 128,
                                     ob * 128:ob * 128 + ow]).then_inc(sm.inc(wl_name[ob % 2], 16), 16)
                        gpsimd.dma_start(
                            out=w1b[0:128, (pb * NCH + kc) * 256 + 128:(pb * NCH + kc) * 256 + 128 + ow],
                            in_=w1_d[kc * 128:(kc + 1) * 128,
                                     HID + ob * 128:HID + ob * 128 + ow]).then_inc(sm.inc(wl_name[ob % 2], 16), 16)

                wl_tgt1[ob] = sm.n[wl_name[ob % 2]]
                for s in range(NS):
                    ti = ti1[0]
                    ti1[0] += 1

                    @block.tensor
                    def _(tensor, ob=ob, ow=ow, s=s, ti=ti):
                        pb = ob % 2
                        if ti == 0:
                            tensor.wait_ge(sm["dve"], lnh_dve)
                        tensor.wait_ge(sm[wl_name[pb]], wl_tgt1[ob])
                        if ti >= 2:
                            tensor.wait_ge(sm["dve"], dve_hf0 + ti - 1)
                        for kc in range(NCH):
                            tensor.matmul(pbank(2 * (ti % 2))[0:ow, :],
                                          w1b[0:128, (pb * NCH + kc) * 256:(pb * NCH + kc) * 256 + ow],
                                          lnc(kc)[:, s * 512:(s + 1) * 512],
                                          start=(kc == 0), stop=(kc == NCH - 1))
                        for kc in range(NCH):
                            ins = tensor.matmul(pbank(2 * (ti % 2) + 1)[0:ow, :],
                                                w1b[0:128, (pb * NCH + kc) * 256 + 128:(pb * NCH + kc) * 256 + 128 + ow],
                                                lnc(kc)[:, s * 512:(s + 1) * 512],
                                                start=(kc == 0), stop=(kc == NCH - 1))
                        ins.then_inc(sm.inc("pe"), 1)

                    @block.scalar
                    def _(scalar, ob=ob, ow=ow, s=s, ti=ti):
                        scalar.wait_ge(sm["pe"], sm.n["pe"])
                        ins = scalar.activation(stg[0:ow, (ti % 2) * 512:(ti % 2 + 1) * 512],
                                                pbank(2 * (ti % 2) + 1)[0:ow, :], AF.Gelu,
                                                bias=b1_sb[0:ow, NHC + ob:NHC + ob + 1])
                        ins.then_inc(sm.inc("act"), 1)

                    @block.vector
                    def _(vector, ob=ob, ow=ow, s=s, ti=ti):
                        vector.wait_ge(sm["act"], sm.n["act"])
                        ins = vector.scalar_tensor_tensor(
                            hf[0:ow, ob * TH + s * 512:ob * TH + (s + 1) * 512],
                            pbank(2 * (ti % 2))[0:ow, :], b1_sb[0:ow, ob:ob + 1],
                            stg[0:ow, (ti % 2) * 512:(ti % 2 + 1) * 512],
                            OP.add, OP.mult)
                        ins.then_inc(sm.inc("dve"), 1)
                pe_after1.append(sm.n["pe"])

            hf_dve = sm.n["dve"]
            pe_after2 = []
            ti2 = [0]
            pe_w2b = sm.n["pe"]
            wl_tgt2 = {}
            for ob in range(D // 256):
                @block.gpsimd
                def _(gpsimd, ob=ob):
                    if ob >= 2:
                        gpsimd.wait_ge(sm["pe"], pe_after2[ob - 2])
                    else:
                        gpsimd.wait_ge(sm["pe"], pe_w2b)
                    pb = ob % 2
                    for kc in range(NHC):
                        kl = 128 if kc < NHC - 1 else HID - 128 * (NHC - 1)
                        gpsimd.dma_start(
                            out=w2b[0:kl, (pb * NHC + kc) * 256:(pb * NHC + kc + 1) * 256],
                            in_=w2_d[kc * 128:kc * 128 + kl,
                                     ob * 256:(ob + 1) * 256]).then_inc(sm.inc(wl_name[ob % 2], 16), 16)
                wl_tgt2[ob] = sm.n[wl_name[ob % 2]]

                for oc in range(2):
                    for s in range(NS):
                        ti = ti2[0]
                        ti2[0] += 1
                        og = ob * 2 + oc

                        @block.tensor
                        def _(tensor, ob=ob, oc=oc, s=s, ti=ti):
                            pb = ob % 2
                            if ti == 0:
                                tensor.wait_ge(sm["dve"], hf_dve)
                            tensor.wait_ge(sm[wl_name[pb]], wl_tgt2[ob])
                            if ti >= 4:
                                tensor.wait_ge(sm["dve"], hf_dve + ti - 3)
                            for kc in range(NHC):
                                kl = 128 if kc < NHC - 1 else HID - 128 * (NHC - 1)
                                ins = tensor.matmul(
                                    pbank(4 + ti % 4)[:],
                                    w2b[0:kl, (pb * NHC + kc) * 256 + oc * 128:(pb * NHC + kc) * 256 + oc * 128 + 128],
                                    hf[0:kl, kc * TH + s * 512:kc * TH + (s + 1) * 512],
                                    start=(kc == 0), stop=(kc == NHC - 1))
                            ins.then_inc(sm.inc("pe"), 1)

                        @block.vector
                        def _(vector, og=og, s=s, ti=ti):
                            vector.wait_ge(sm["pe"], sm.n["pe"])
                            dst = xmc(og)[:, 1 + t0 + s * 512:1 + t0 + (s + 1) * 512]
                            ins = vector.scalar_tensor_tensor(
                                dst, pbank(4 + ti % 4)[:], b2_sb[:, og:og + 1],
                                dst, OP.add, OP.add)
                            ins.then_inc(sm.inc("dve"), 1)
                pe_after2.append(sm.n["pe"])

        @block.sync
        def _(sync):
            sync.wait_ge(sm["dve"], sm.n["dve"])
            sync.dma_start(
                out=yn_d.rearrange("(c p) t -> p c t", p=128),
                in_=xm_sb.ap().rearrange("p (c t) -> p c t", t=1 + TPC)[:, :, 1:]).then_inc(sm.inc("st", 16), 16)
            sync.wait_ge(sm["st"], sm.n["st"])
    return nc


def c2_host(xm_full, wqkv_l, lg, lb, w1, b1, w2, b2):
    HID = 2730
    NHC = 22
    lgb = np.zeros((128, 16), np.float32)
    lgb[:, 0:8] = lg.reshape(8, 128).T
    lgb[:, 8:16] = lb.reshape(8, 128).T
    b1m = np.zeros((128, 2 * NHC), np.float32)
    b1a = b1[:HID]
    b1g = b1[HID:]
    for oc in range(NHC):
        w = min(128, HID - oc * 128)
        b1m[0:w, oc] = b1a[oc * 128:oc * 128 + w]
        b1m[0:w, NHC + oc] = b1g[oc * 128:oc * 128 + w]
    b2m = np.ascontiguousarray(b2.reshape(8, 128).T)
    ins = []
    for c in range(NCORE):
        b, p = c // 2, c % 2
        lo = p * TPC
        xm = np.zeros((D, 1 + TPC), np.float32)
        if lo > 0:
            xm[:, 0] = xm_full[b, lo - 1]
        xm[:, 1:] = xm_full[b, lo:lo + TPC].T
        ins.append({"xm": xm, "w1": np.ascontiguousarray(w1, np.float32),
                    "w2": np.ascontiguousarray(w2, np.float32),
                    "lgb": lgb, "b1": b1m, "b2": b2m})
    res = _run("c2", build_c2, ins)
    yn = np.empty((B, S, D), np.float32)
    for c in range(NCORE):
        b, p = c // 2, c % 2
        yn[b, p * TPC:(p + 1) * TPC] = res.results[c]["ynext"].T
    return yn



NEG1 = -3.0e37


def build_c1():
    import os
    NH_DBG = int(os.environ.get("C1_NH", "8"))
    WO_DBG = os.environ.get("C1_WO", "1") == "1"
    STG = int(os.environ.get("C1_STAGE", "5"))
    AT_DBG = os.environ.get("C1_AT", "1") == "1"
    ATP = int(os.environ.get("C1_ATP", "4"))
    HLO = 513
    YW = HLO + TPC
    KW = 512 + TPC
    NKC1 = KW // 128
    NST5 = KW // 512
    nc = bass.Bass("TRN2", target_bir_lowering=False)
    y_d = nc.dram_tensor("y", [D, YW], F32, kind="ExternalInput").ap()
    wqkv_d = nc.dram_tensor("wqkv", [D, 1536], F32, kind="ExternalInput").ap()
    wo_d = nc.dram_tensor("wo", [512, D], F32, kind="ExternalInput").ap()
    rqa_d = nc.dram_tensor("rqa", [64, TPC], F32, kind="ExternalInput").ap()
    rqb_d = nc.dram_tensor("rqb", [64, TPC], F32, kind="ExternalInput").ap()
    rka_d = nc.dram_tensor("rka", [64, KW], F32, kind="ExternalInput").ap()
    rkb_d = nc.dram_tensor("rkb", [64, KW], F32, kind="ExternalInput").ap()
    qsks_d = nc.dram_tensor("qsks", [64, 2], F32, kind="ExternalInput").ap()
    hpen_d = nc.dram_tensor("hpen", [128, 512], F32, kind="ExternalInput").ap()
    xm_d = nc.dram_tensor("xmid", [D, TPC], F32, kind="ExternalOutput").ap()
    DBG = os.environ.get("C1_DBG", "0") == "1"
    DBG2 = os.environ.get("C1_DBG2", "0") == "1"
    if DBG:
        av_d = nc.dram_tensor("av_dbg", [65, 512], F32, kind="ExternalOutput").ap()
        pb_d = nc.dram_tensor("pb_dbg", [128, 4 * 512], F32, kind="ExternalOutput").ap()
        qa_d = nc.dram_tensor("qa_dbg", [65, TPC], F32, kind="ExternalOutput").ap()
        ka_d = nc.dram_tensor("ka_dbg", [65, 512 + TPC], F32, kind="ExternalOutput").ap()
        ka2_d = nc.dram_tensor("ka2_dbg", [65, 512 + TPC], F32, kind="ExternalOutput").ap()
        ka3_d = nc.dram_tensor("ka3_dbg", [65, 512 + TPC], BF16, kind="ExternalOutput").ap()
        o_d = nc.dram_tensor("o_dbg", [128, 4 * TPC], F32, kind="ExternalOutput").ap()

    es = ExitStack()
    with es:
        y_bf = es.enter_context(nc.sbuf_tensor("y_bf", [128, NCH * YW], BF16))
        wqkv_sb = es.enter_context(nc.sbuf_tensor("wqkv_sb", [128, NCH * 1536], BF16))
        wob = es.enter_context(nc.sbuf_tensor("wob", [128, 2 * 4 * 128], BF16))
        yown = es.enter_context(nc.sbuf_tensor("yown", [128, 2 * TPC], F32))
        qa = es.enter_context(nc.sbuf_tensor("qa", [65, TPC], BF16))
        ka = es.enter_context(nc.sbuf_tensor("ka", [65, KW], BF16))
        vt = es.enter_context(nc.sbuf_tensor("vt", [128, NKC1 * 65], BF16))
        o_sb = es.enter_context(nc.sbuf_tensor("o_sb", [128, 4 * TPC], BF16))
        psb = es.enter_context(nc.sbuf_tensor("psb", [128, 4 * 512], BF16))
        stgf = es.enter_context(nc.sbuf_tensor("stgf", [128, 2 * 512], F32))
        rqa = es.enter_context(nc.sbuf_tensor("rqa_sb", [64, TPC], F32))
        rqb = es.enter_context(nc.sbuf_tensor("rqb_sb", [64, TPC], F32))
        rka = es.enter_context(nc.sbuf_tensor("rka_sb", [64, KW], F32))
        rkb = es.enter_context(nc.sbuf_tensor("rkb_sb", [64, KW], F32))
        qsks = es.enter_context(nc.sbuf_tensor("qsks_sb", [64, 2], F32))
        hpen = es.enter_context(nc.sbuf_tensor("hpen_sb", [128, 512], F32))
        msk = es.enter_context(nc.sbuf_tensor("msk", [128, 4 * 512], F32))
        mskt = es.enter_context(nc.sbuf_tensor("mskt", [128, 4 * 512], F32))
        sqk = es.enter_context(nc.sbuf_tensor("sqk", [64, 2 * 512], BF16))
        msks = es.enter_context(nc.sbuf_tensor("msks", [128, 2 * 512], F32))
        t1 = es.enter_context(nc.sbuf_tensor("t1", [64, 512], F32))
        t2 = es.enter_context(nc.sbuf_tensor("t2", [64, 2 * 512], F32))
        rm = es.enter_context(nc.sbuf_tensor("rm", [128, 2], F32))
        nm1 = es.enter_context(nc.sbuf_tensor("nm1", [128, 1], BF16))
        rrow = es.enter_context(nc.sbuf_tensor("rrow", [1, 2 * 512], F32))
        rec = es.enter_context(nc.sbuf_tensor("rec", [1, 512], F32))
        rb_sb = es.enter_context(nc.sbuf_tensor("rb_sb", [64, 512], F32))
        ones1 = es.enter_context(nc.sbuf_tensor("ones1", [1, 128], F32))
        onesb = es.enter_context(nc.sbuf_tensor("onesb", [64, 1], BF16))
        ident = es.enter_context(nc.sbuf_tensor("ident", [128, 128], BF16))
        ps = es.enter_context(nc.psum_tensor("ps", [128, 8 * 512], F32))
        sm = Sems(nc, es, ["ld", "wld", "wla", "wlb", "yla", "ylb", "pe", "act", "dve", "st", "sta", "stb", "gst"])
        block = es.enter_context(nc.Block())

        pbank = lambda i: ps[:, i * 512:(i + 1) * 512]
        ybc = lambda c: y_bf[:, c * YW:(c + 1) * YW]
        ysh = lambda c, col0, w: (ybc(c)[:, col0:col0 + w] if c < 4
                                  else ybc(c)[:, col0 - 1:col0 - 1 + w])

        msk_np = np.zeros((128, 4 * 512), np.float32)
        mskt_np = np.zeros((128, 4 * 512), np.float32)
        ii = np.arange(128)[:, None]
        jj = np.arange(512)[None, :]
        for v in range(4):
            msk_np[:, v * 512:(v + 1) * 512] = np.where(jj <= v * 128 + ii, 0.0, NEG1)
            mskt_np[:, v * 512:(v + 1) * 512] = np.where(jj >= v * 128 + ii, 0.0, NEG1)
        mskd = nc.inline_tensor(msk_np, "mskd").ap()
        msktd = nc.inline_tensor(mskt_np, "msktd").ap()
        identd = nc.inline_tensor(np.eye(128).astype(ml_dtypes.bfloat16), "identd").ap()

        @block.sync
        def _(sync):
            for dst, src in ((msk, mskd), (mskt, msktd), (ident, identd),
                             (rqa, rqa_d), (rqb, rqb_d), (rka, rka_d),
                             (rkb, rkb_d), (qsks, qsks_d), (hpen, hpen_d)):
                sync.dma_start(out=dst[:], in_=src[:]).then_inc(sm.inc("ld", 16), 16)

        @block.gpsimd
        def _(gpsimd):
            gpsimd.dma_start(out=y_bf.ap().rearrange("p (c t) -> p c t", t=YW),
                             in_=y_d.rearrange("(c p) t -> p c t", p=128)).then_inc(sm.inc("wld", 16), 16)
            gpsimd.dma_start(out=wqkv_sb.ap().rearrange("p (c w) -> p c w", w=1536),
                             in_=wqkv_d.rearrange("(c p) w -> p c w", p=128)).then_inc(sm.inc("wld", 16), 16)

        @block.vector
        def _(vector):
            vector.memset(ones1[:], 1.0)
            vector.memset(onesb[:], 1.0)
            vector.memset(ka[64:65, :], 1.0)
            ins = vector.memset(vt[:, 64::65], 1.0)
            ins.then_inc(sm.inc("dve"), 1)

        ld0 = sm.n["ld"]


        for h in range(NH_DBG):
            kcol = 512 + h * 64
            vcol = 1024 + h * 64
            qcol = h * 64

            act_kb = sm.n["act"]
            for st in range(NST5):
                @block.tensor
                def _(tensor, st=st, actn=sm.n["act"]):
                    if st == 0:
                        tensor.wait_ge(sm["ld"], ld0)
                        tensor.wait_ge(sm["wld"], 32)
                        tensor.wait_ge(sm["dve"], sm.n["dve"])
                    if st < 2:
                        tensor.wait_ge(sm["act"], actn)
                    else:
                        tensor.wait_ge(sm["act"], act_kb + st - 1)
                    for kc in range(NCH):
                        ins = tensor.matmul(pbank(6 + st % 2)[0:64, :],
                                            wqkv_sb[:, kc * 1536 + kcol:kc * 1536 + kcol + 64],
                                            ysh(kc, 1 + st * 512, 512),
                                            start=(kc == 0), stop=(kc == NCH - 1))
                    ins.then_inc(sm.inc("pe"), 1)

                @block.scalar
                def _(scalar, st=st):
                    scalar.wait_ge(sm["pe"], sm.n["pe"])
                    ins = scalar.activation(ka[0:64, st * 512:(st + 1) * 512],
                                            pbank(6 + st % 2)[0:64, :], AF.Copy)
                    ins.then_inc(sm.inc("act"), 1)

            def barrier(eng):
                eng.wait_ge(sm["pe"], sm.n["pe"])
                eng.wait_ge(sm["act"], sm.n["act"])
                eng.wait_ge(sm["dve"], sm.n["dve"])

            def norm_rot(dst, tabA, tabB, scol, nst):
                for st in range(nst):
                    sl = slice(st * 512, (st + 1) * 512)

                    @block.scalar
                    def _(scalar, st=st, sl=sl):
                        barrier(scalar)
                        ins = scalar.activation(
                            sqk[:, (st % 2) * 512:(st % 2 + 1) * 512],
                            dst[0:64, sl], AF.Square)
                        ins.then_inc(sm.inc("act"), 1)

                    @block.tensor
                    def _(tensor, st=st):
                        barrier(tensor)
                        ins = tensor.matmul(pbank(5)[0:1, :], onesb[:, 0:1],
                                            sqk[:, (st % 2) * 512:(st % 2 + 1) * 512],
                                            start=True, stop=True)
                        ins.then_inc(sm.inc("pe"), 1)

                    @block.vector
                    def _(vector, st=st):
                        barrier(vector)
                        ins = vector.tensor_copy(rrow[0:1, 0:512], pbank(5)[0:1, :])
                        ins.then_inc(sm.inc("dve"), 1)

                    @block.scalar
                    def _(scalar, st=st):
                        barrier(scalar)
                        ins = scalar.activation(rrow[0:1, 512:1024], rrow[0:1, 0:512],
                                                AF.Sqrt)
                        ins.then_inc(sm.inc("act"), 1)

                    @block.vector
                    def _(vector, st=st):
                        barrier(vector)
                        _ss(vector, sm, vector.tensor_scalar_max(
                            rrow[0:1, 512:1024], rrow[0:1, 512:1024], 1e-12))
                        ins = vector.reciprocal(rrow[0:1, 0:512],
                                                rrow[0:1, 512:1024])
                        ins.then_inc(sm.inc("dve"), 1)

                    @block.tensor
                    def _(tensor, st=st):
                        barrier(tensor)
                        ins = tensor.matmul(pbank(5)[0:64, :], ones1[0:1, 0:64],
                                            rrow[0:1, 0:512], start=True, stop=True)
                        ins.then_inc(sm.inc("pe"), 1)

                    @block.vector
                    def _(vector, st=st, sl=sl):
                        barrier(vector)
                        _ss(vector, sm, vector.tensor_scalar_mul(
                            t1[:, 0:512], dst[0:64, sl], scol))
                        _ss(vector, sm, vector.tensor_mul(
                            t1[:, 0:512], t1[:, 0:512], pbank(5)[0:64, :]))
                        _ss(vector, sm, vector.tensor_mul(
                            t2[0:32, 0:512], t1[0:32, :], tabA[0:32, sl]))
                        _ss(vector, sm, vector.tensor_mul(
                            t2[0:32, 512:1024], t1[32:64, :], tabA[32:64, sl]))
                        _ss(vector, sm, vector.tensor_sub(
                            dst[0:32, sl], t2[0:32, 0:512], t2[0:32, 512:1024]))
                        _ss(vector, sm, vector.tensor_mul(
                            t2[32:64, 0:512], t1[0:32, :], tabB[0:32, sl]))
                        _ss(vector, sm, vector.tensor_mul(
                            t2[32:64, 512:1024], t1[32:64, :], tabB[32:64, sl]))
                        ins = vector.tensor_add(dst[32:64, sl], t2[32:64, 0:512],
                                                t2[32:64, 512:1024])
                        ins.then_inc(sm.inc("dve"), 1)

            norm_rot(ka, rka, rkb, qsks[:, 1:2], NST5)

            if DBG and h == 0:
                @block.gpsimd
                def _(gpsimd):
                    gpsimd.wait_ge(sm["dve"], sm.n["dve"])
                    gpsimd.dma_start(out=ka2_d[:], in_=ka[:]).then_inc(sm.inc("gst", 16), 16)

                @block.sync
                def _(sync):
                    sync.wait_ge(sm["dve"], sm.n["dve"])
                    sync.dma_start(out=ka3_d[:], in_=ka[:]).then_inc(sm.inc("st", 16), 16)

            dve_vb = sm.n["dve"]
            for tc in range(NKC1 if STG >= 3 else 0):
                @block.tensor
                def _(tensor, tc=tc, actn=sm.n["act"]):
                    if tc < 2:
                        tensor.wait_ge(sm["act"], actn)
                    else:
                        tensor.wait_ge(sm["dve"], dve_vb + tc - 1)
                    for kc in range(NCH):
                        ins = tensor.matmul(pbank(6 + tc % 2)[:, 0:64],
                                            ysh(kc, 1 + tc * 128, 128),
                                            wqkv_sb[:, kc * 1536 + vcol:kc * 1536 + vcol + 64],
                                            start=(kc == 0), stop=(kc == NCH - 1))
                    ins.then_inc(sm.inc("pe"), 1)

                @block.vector
                def _(vector, tc=tc):
                    vector.wait_ge(sm["pe"], sm.n["pe"])
                    ins = vector.tensor_copy(vt[:, tc * 65:tc * 65 + 64],
                                             pbank(6 + tc % 2)[:, 0:64])
                    ins.then_inc(sm.inc("dve"), 1)

            act_qb = sm.n["act"]
            for st in range(4 if STG >= 4 else 0):
                @block.tensor
                def _(tensor, st=st, dven=sm.n["dve"]):
                    if st < 2:
                        tensor.wait_ge(sm["dve"], dven)
                    else:
                        tensor.wait_ge(sm["act"], act_qb + st - 1)
                    for kc in range(NCH):
                        ins = tensor.matmul(pbank(6 + st % 2)[0:64, :],
                                            wqkv_sb[:, kc * 1536 + qcol:kc * 1536 + qcol + 64],
                                            ysh(kc, HLO + st * 512, 512),
                                            start=(kc == 0), stop=(kc == NCH - 1))
                    ins.then_inc(sm.inc("pe"), 1)

                @block.scalar
                def _(scalar, st=st):
                    scalar.wait_ge(sm["pe"], sm.n["pe"])
                    ins = scalar.activation(qa[0:64, st * 512:(st + 1) * 512],
                                            pbank(6 + st % 2)[0:64, :], AF.Copy)
                    ins.then_inc(sm.inc("act"), 1)

            norm_rot(qa, rqa, rqb, qsks[:, 0:1], 4)

            qk_dve = sm.n["dve"]

            for w in range(4 if AT_DBG else 0):
                for sub in range(4):
                    qt = w * 4 + sub
                    dve_sb = sm.n["dve"]
                    for j in range(2):
                        @block.tensor
                        def _(tensor, qt=qt, j=j, w=w, dve_sb=dve_sb):
                            if j < 2:
                                tensor.wait_ge(sm["dve"], dve_sb)
                            ins = tensor.matmul(pbank(j % 2)[:],
                                                qa[0:64, qt * 128:(qt + 1) * 128],
                                                ka[0:64, (w + j) * 512:(w + j + 1) * 512],
                                                start=True, stop=True)
                            ins.then_inc(sm.inc("pe"), 1)


                        @block.vector
                        def _(vector, qt=qt, j=j, w=w, sub=sub):
                            vector.wait_ge(sm["pe"], sm.n["pe"])
                            vector.wait_ge(sm["dve"], sm.n["dve"])
                            bank = pbank(j % 2)
                            ms = msks[:, 0:512]
                            if j == 0:
                                _ss(vector, sm, vector.tensor_add(
                                    ms, bank[:], mskt[:, sub * 512:(sub + 1) * 512]))
                                if w == 0:
                                    _ss(vector, sm, vector.tensor_add(ms, ms,
                                                                      hpen[:, 0:512]))
                                ins = vector.tensor_reduce(rm[:, 0:1], ms,
                                                           axis=AX.X, op=OP.max)
                            else:
                                _ss(vector, sm, vector.tensor_add(
                                    ms, bank[:], msk[:, sub * 512:(sub + 1) * 512]))
                                _ss(vector, sm, vector.tensor_reduce(
                                    rm[:, 1:2], ms, axis=AX.X, op=OP.max))
                                ins = vector.tensor_max(rm[:, 0:1], rm[:, 0:1],
                                                        rm[:, 1:2])
                            ins.then_inc(sm.inc("dve"), 1)

                    if DBG2 and h == 0 and w == 0 and sub == 0:
                        @block.vector
                        def _(vector):
                            vector.tensor_copy(t2[0:64, 0:512], pbank(1)[0:64, :])
                            ins = vector.tensor_copy(t1[0:64, 0:512], rm[:, 0:2][0:64, :]) if False else vector.tensor_copy(t2[0:64, 0:512], pbank(1)[0:64, :])
                            ins.then_inc(sm.inc("dve"), 1)

                        @block.gpsimd
                        def _(gpsimd):
                            gpsimd.wait_ge(sm["dve"], sm.n["dve"])
                            gpsimd.dma_start(out=pb_d[64:128, 512:1024],
                                             in_=t2[0:64, 0:512]).then_inc(sm.inc("st", 16), 16)

                    if DBG2 and h == 0 and w == 0 and sub == 0:
                        @block.gpsimd
                        def _(gpsimd):
                            gpsimd.wait_ge(sm["dve"], sm.n["dve"])
                            gpsimd.dma_start(out=av_d[0:64, :],
                                             in_=msks[0:64, 0:512]).then_inc(sm.inc("gst", 16), 16)
                            gpsimd.dma_start(out=av_d[64:65, 0:1],
                                             in_=rm[0:1, 0:1]).then_inc(sm.inc("gst", 16), 16)

                        @block.vector
                        def _(vector):
                            vector.wait_ge(sm["gst"], sm.n["gst"])

                        @block.tensor
                        def _(tensor):
                            tensor.wait_ge(sm["gst"], sm.n["gst"])

                        @block.scalar
                        def _(scalar):
                            scalar.wait_ge(sm["gst"], sm.n["gst"])

                    @block.vector
                    def _(vector, qt=qt):
                        vector.wait_ge(sm["dve"], sm.n["dve"])
                        ins = vector.tensor_scalar_mul(nm1[:, 0:1], rm[:, 0:1], -1.0)
                        ins.then_inc(sm.inc("dve"), 1)

                    @block.tensor
                    def _(tensor, qt=qt):
                        tensor.wait_ge(sm["dve"], sm.n["dve"])
                        ins = tensor.matmul(pbank(5)[0:1, 0:128], nm1[:, 0:1],
                                            ident[:], start=True, stop=True)
                        ins.then_inc(sm.inc("pe"), 1)

                    @block.vector
                    def _(vector, qt=qt):
                        vector.wait_ge(sm["pe"], sm.n["pe"])
                        ins = vector.tensor_copy(qa[64:65, qt * 128:(qt + 1) * 128],
                                                 pbank(5)[0:1, 0:128])
                        ins.then_inc(sm.inc("dve"), 1)

                if ATP < 2:
                    continue
                row_dve = sm.n["dve"]
                act_stb = sm.n["act"]
                st_pe = {}
                av_pe = {}
                exp_act = {}
                msk_dve = {}

                def emit_st(c, w=w):
                    @block.tensor
                    def _(tensor, c=c):
                        tensor.wait_ge(sm["dve"], row_dve)
                        if c >= 2:
                            tensor.wait_ge(sm["dve"], msk_dve[c - 2])
                        ins = tensor.matmul(pbank(2 + c % 2)[:],
                                            ka[:, (w * 4 + c) * 128:(w * 4 + c + 1) * 128],
                                            qa[:, w * 512:(w + 1) * 512],
                                            start=True, stop=True)
                        ins.then_inc(sm.inc("pe"), 1)
                    st_pe[c] = sm.n["pe"]

                    @block.vector
                    def _(vector, c=c, w=w):
                        vector.wait_ge(sm["pe"], st_pe[c])
                        vector.wait_ge(sm["dve"], sm.n["dve"])
                        if c >= 2:
                            vector.wait_ge(sm["act"], exp_act[c - 2])
                        bank = pbank(2 + c % 2)
                        ms = msks[:, (c % 2) * 512:(c % 2 + 1) * 512]
                        if c < 4:
                            ins = vector.tensor_add(ms, bank[:],
                                                    msk[:, c * 512:(c + 1) * 512])
                            if w == 0:
                                _ss(vector, sm, ins)
                                ins = vector.tensor_add(ms, ms, hpen[:, 0:512])
                        else:
                            ins = vector.tensor_add(ms, bank[:],
                                                    mskt[:, (c - 4) * 512:(c - 3) * 512])
                        ins.then_inc(sm.inc("dve"), 1)
                    msk_dve[c] = sm.n["dve"]

                    if DBG2 and h == 0 and w == 0 and c == 4:
                        @block.vector
                        def _(vector, c=c):
                            vector.wait_ge(sm["dve"], msk_dve[c])
                            ins = vector.tensor_copy(t2[0:64, 0:512],
                                                     pbank(2 + c % 2)[0:64, :])
                            ins.then_inc(sm.inc("dve"), 1)

                        @block.gpsimd
                        def _(gpsimd):
                            gpsimd.wait_ge(sm["dve"], sm.n["dve"])
                            gpsimd.dma_start(out=pb_d[0:64, 0:512],
                                             in_=t2[0:64, 0:512]).then_inc(sm.inc("gst", 16), 16)

                    @block.scalar
                    def _(scalar, c=c, extra=(sm.n["gst"] if (DBG2 and h == 0 and w == 0 and c == 4) else None)):
                        scalar.wait_ge(sm["dve"], msk_dve[c])
                        if extra is not None:
                            scalar.wait_ge(sm["gst"], extra)
                        if c >= 4:
                            scalar.wait_ge(sm["pe"], av_pe[c - 4])
                        ins = scalar.activation(psb[:, (c % 4) * 512:(c % 4 + 1) * 512],
                                                msks[:, (c % 2) * 512:(c % 2 + 1) * 512],
                                                AF.Exp)
                        ins.then_inc(sm.inc("act"), 1)
                    exp_act[c] = sm.n["act"]

                def emit_av(c, w=w):
                    @block.tensor
                    def _(tensor, c=c, w=w):
                        tensor.wait_ge(sm["act"], exp_act[c])
                        ins = tensor.matmul(pbank(4)[0:65, :],
                                            vt[:, (w * 4 + c) * 65:(w * 4 + c + 1) * 65],
                                            psb[:, (c % 4) * 512:(c % 4 + 1) * 512],
                                            start=(c == 0), stop=(c == 7))
                        ins.then_inc(sm.inc("pe"), 1)
                    av_pe[c] = sm.n["pe"]

                if ATP >= 3:
                    for c in range(8):
                        emit_st(c)
                        if c >= 1:
                            emit_av(c - 1)
                    emit_av(7)
                else:
                    for c in range(8):
                        emit_st(c)

                if DBG2 and h == 0 and w == 0:
                    @block.vector
                    def _(vector):
                        vector.wait_ge(sm["pe"], sm.n["pe"])
                        ins = vector.tensor_copy(t2[0:64, 0:512], pbank(4)[0:64, :])
                        ins.then_inc(sm.inc("dve"), 1)

                    @block.gpsimd
                    def _(gpsimd):
                        gpsimd.wait_ge(sm["dve"], sm.n["dve"])
                        gpsimd.dma_start(out=av_d[0:64, :], in_=t2[0:64, 0:512]).then_inc(sm.inc("st", 16), 16)
                        gpsimd.dma_start(out=pb_d[:], in_=psb[:]).then_inc(sm.inc("st", 16), 16)

                if ATP < 4:
                    continue

                @block.vector
                def _(vector):
                    vector.wait_ge(sm["pe"], sm.n["pe"])
                    ins = vector.reciprocal(rec[:], pbank(4)[64:65, :])
                    ins.then_inc(sm.inc("dve"), 1)

                if DBG2 and h == 0 and w == 0:
                    @block.vector
                    def _(vector):
                        ins = vector.tensor_copy(rrow[0:1, 0:512], rec[:])
                        ins.then_inc(sm.inc("dve"), 1)

                    @block.gpsimd
                    def _(gpsimd):
                        gpsimd.wait_ge(sm["dve"], sm.n["dve"])
                        gpsimd.dma_start(out=av_d[64:65, :], in_=rrow[0:1, 0:512]).then_inc(sm.inc("st", 16), 16)

                @block.tensor
                def _(tensor):
                    tensor.wait_ge(sm["dve"], sm.n["dve"])
                    ins = tensor.matmul(pbank(5)[0:64, :], ones1[0:1, 0:64],
                                        rec[0:1, :], start=True, stop=True)
                    ins.then_inc(sm.inc("pe"), 1)

                @block.vector
                def _(vector, w=w, h=h):
                    och = h // 2
                    orow = (h % 2) * 64
                    vector.wait_ge(sm["pe"], sm.n["pe"])
                    vector.wait_ge(sm["dve"], sm.n["dve"])
                    _ss(vector, sm, vector.tensor_copy(rb_sb[:],
                                                       pbank(5)[0:64, :]))
                    ins = vector.tensor_mul(
                        o_sb[orow:orow + 64, och * TPC + w * 512:och * TPC + (w + 1) * 512],
                        pbank(4)[0:64, :], rb_sb[:])
                    ins.then_inc(sm.inc("dve"), 1)

        attn_dve = sm.n["dve"]
        oc_ = lambda c: o_sb[:, c * TPC:(c + 1) * TPC]
        pe_after = []
        ti3 = [0]
        st_b = sm.n["st"]
        pe_wo0 = sm.n["pe"]
        wl_name = ["wla", "wlb"]
        yl_name = ["yla", "ylb"]
        st_name = ["sta", "stb"]
        wl_tgt = {}
        yl_tgt = {}
        st_tgt = {}
        for ob in range(NCH if WO_DBG else 0):
            @block.gpsimd
            def _(gpsimd, ob=ob):
                if ob >= 2:
                    gpsimd.wait_ge(sm["pe"], pe_after[ob - 2])
                else:
                    gpsimd.wait_ge(sm["pe"], pe_wo0)
                pb = ob % 2
                for kc in range(4):
                    gpsimd.dma_start(
                        out=wob[:, (pb * 4 + kc) * 128:(pb * 4 + kc + 1) * 128],
                        in_=wo_d[kc * 128:(kc + 1) * 128,
                                 ob * 128:(ob + 1) * 128]).then_inc(sm.inc(wl_name[pb], 16), 16)
            wl_tgt[ob] = sm.n[wl_name[ob % 2]]

            @block.sync
            def _(sync, ob=ob):
                if ob >= 2:
                    sync.wait_ge(sm["dve"], attn_dve + (ob - 1) * 4)
                sync.dma_start(out=yown[:, (ob % 2) * TPC:(ob % 2 + 1) * TPC],
                               in_=y_d[ob * 128:(ob + 1) * 128, HLO:]).then_inc(sm.inc(yl_name[ob % 2], 16), 16)
            yl_tgt[ob] = sm.n[yl_name[ob % 2]]

            for s in range(4):
                ti = ti3[0]
                ti3[0] += 1

                @block.tensor
                def _(tensor, ob=ob, s=s, ti=ti):
                    pb = ob % 2
                    if ti == 0:
                        tensor.wait_ge(sm["dve"], attn_dve)
                    tensor.wait_ge(sm[wl_name[pb]], wl_tgt[ob])
                    if ti >= 4:
                        tensor.wait_ge(sm["dve"], attn_dve + ti - 3)
                    for kc in range(4):
                        ins = tensor.matmul(pbank(ti % 4)[:],
                                            wob[:, (pb * 4 + kc) * 128:(pb * 4 + kc + 1) * 128],
                                            oc_(kc)[:, s * 512:(s + 1) * 512],
                                            start=(kc == 0), stop=(kc == 3))
                    ins.then_inc(sm.inc("pe"), 1)

                @block.vector
                def _(vector, ob=ob, s=s, ti=ti):
                    vector.wait_ge(sm["pe"], sm.n["pe"])
                    vector.wait_ge(sm[yl_name[ob % 2]], yl_tgt[ob])
                    if ti >= 2:
                        vector.wait_ge(sm[st_name[ti % 2]], st_tgt[ti - 2])
                    ins = vector.tensor_add(stgf[:, (ti % 2) * 512:(ti % 2 + 1) * 512],
                                            pbank(ti % 4)[:],
                                            yown[:, (ob % 2) * TPC + s * 512:(ob % 2) * TPC + (s + 1) * 512])
                    ins.then_inc(sm.inc("dve"), 1)

                @block.sync
                def _(sync, ob=ob, s=s, ti=ti):
                    sync.wait_ge(sm["dve"], sm.n["dve"])
                    sync.dma_start(out=xm_d[ob * 128:(ob + 1) * 128, s * 512:(s + 1) * 512],
                                   in_=stgf[:, (ti % 2) * 512:(ti % 2 + 1) * 512]).then_inc(sm.inc(st_name[ti % 2], 16), 16)
                st_tgt[ti] = sm.n[st_name[ti % 2]]
            pe_after.append(sm.n["pe"])

        if DBG:
            @block.gpsimd
            def _(gpsimd):
                gpsimd.wait_ge(sm["dve"], sm.n["dve"])
                gpsimd.dma_start(out=pb_d[64:128, 0:512], in_=rka[0:64, 512:1024]).then_inc(sm.inc("gst", 16), 16)
                gpsimd.dma_start(out=pb_d[64:128, 512:1024], in_=rkb[0:64, 512:1024]).then_inc(sm.inc("gst", 16), 16)
                gpsimd.dma_start(out=qa_d[:], in_=qa[:]).then_inc(sm.inc("gst", 16), 16)
                gpsimd.dma_start(out=ka_d[:], in_=ka[:]).then_inc(sm.inc("gst", 16), 16)
                gpsimd.dma_start(out=o_d[:], in_=o_sb[:]).then_inc(sm.inc("gst", 16), 16)

        @block.sync
        def _(sync):
            sync.wait_ge(sm["st"], sm.n["st"])
            sync.wait_ge(sm["sta"], sm.n["sta"])
            sync.wait_ge(sm["stb"], sm.n["stb"])
            sync.wait_ge(sm["gst"], sm.n["gst"])
    return nc


def _rotary_tables(pos, baked_scale, invert):
    DIM_HEAD, WINDOW = 64, 512
    half = np.arange(0, DIM_HEAD, 2, dtype=np.float64)
    inv_freq = 1.0 / 10000.0 ** (half / DIM_HEAD)
    t = pos.astype(np.float64)
    freqs = t[:, None] * inv_freq[None, :]
    base = (half + 0.4 * DIM_HEAD) / (1.4 * DIM_HEAD)
    power = (t - S // 2) / float(WINDOW // 2)
    scale = base[None, :] ** power[:, None]
    if invert:
        scale = 1.0 / scale
    cos = np.cos(freqs) * scale * baked_scale
    sin = np.sin(freqs) * scale * baked_scale
    A = np.concatenate([cos.T, sin.T], 0).astype(np.float32)
    Bm = np.concatenate([sin.T, cos.T], 0).astype(np.float32)
    return A, Bm


def c1_host(y_full, wqkv, qs, ks, wo):
    HLO = 513
    ins = []
    for c in range(NCORE):
        b, p = c // 2, c % 2
        lo = p * TPC
        y = np.zeros((D, HLO + TPC), np.float32)
        if lo >= HLO:
            y[:, :] = y_full[b, lo - HLO:lo + TPC].T
        else:
            y[:, HLO - lo:] = y_full[b, 0:lo + TPC].T
        qpos = np.arange(lo, lo + TPC)
        kpos = np.arange(lo - 512, lo + TPC)
        kpos = np.clip(kpos, 0, S - 1)
        rqa, rqb = _rotary_tables(qpos, 8.0, False)
        rka, rkb = _rotary_tables(kpos, 1.0, True)
        qsks = np.stack([qs, ks], 1).astype(np.float32)
        hpen = np.full((128, 512), NEG1 if p == 0 else 0.0, np.float32)
        ins.append({"y": y, "wqkv": np.ascontiguousarray(wqkv, np.float32),
                    "wo": np.ascontiguousarray(wo, np.float32),
                    "rqa": rqa, "rqb": rqb, "rka": rka, "rkb": rkb,
                    "qsks": qsks, "hpen": hpen})
    res = _run("c1", build_c1, ins)
    xm = np.empty((B, S, D), np.float32)
    for c in range(NCORE):
        b, p = c // 2, c % 2
        xm[b, p * TPC:(p + 1) * TPC] = res.results[c]["xmid"].T
    return xm



def _fm(a):
    return np.ascontiguousarray(a.T, dtype=np.float32)


def _run(name, build_fn, in_maps, **kw):
    if name not in _prog_cache:
        _prog_cache[name] = build_fn()
    return run_bass_kernel_spmd(_prog_cache[name], in_maps,
                                core_ids=list(range(NCORE)), **kw)


def glue_host(x_full, bo_full, emit_h):
    ins = []
    for c in range(NCORE):
        b, p = c // 2, c % 2
        sl = slice(p * TPC, (p + 1) * TPC)
        ins.append({"x": _fm(x_full[b, sl]), "bo": _fm(bo_full[b, sl])})
    res = _run("glue%d" % emit_h, lambda: build_glue(emit_h), ins)
    xn = np.empty((B, S, D), np.float32)
    hn = np.empty((B, S, D), np.float32) if emit_h else None
    for c in range(NCORE):
        b, p = c // 2, c % 2
        sl = slice(p * TPC, (p + 1) * TPC)
        xn[b, sl] = res.results[c]["xn"].T
        if emit_h:
            hn[b, sl] = res.results[c]["hn"].T
    return xn, hn


def kernel(**inputs):
    x = np.asarray(inputs["L0_attn_hidden_states"], np.float32)
    wqkv0 = np.asarray(inputs["L0_attn_W_qkv"], np.float32)
    wo0 = np.asarray(inputs["L0_attn_W_o"], np.float32)

    attn0 = attn_host("l0", x, wqkv0[:, 0:1024], wqkv0[:, 1024:1280],
                      wqkv0[:, 1280:1536])
    x1, y = projglue_host(x, attn0, wo0)

    for l in range(2):
        xm = c1_host(y, inputs["L0_ffn_wqkv"][l],
                     np.asarray(inputs["L0_ffn_q_scale"][l], np.float32),
                     np.asarray(inputs["L0_ffn_k_scale"][l], np.float32),
                     inputs["L0_ffn_wo"][l])
        y = c2_host(xm, None,
                    np.asarray(inputs["L0_ffn_ln_g"][l], np.float32),
                    np.asarray(inputs["L0_ffn_ln_b"][l], np.float32),
                    np.asarray(inputs["L0_ffn_w1"][l], np.float32),
                    np.asarray(inputs["L0_ffn_b1"][l], np.float32),
                    np.asarray(inputs["L0_ffn_w2"][l], np.float32),
                    np.asarray(inputs["L0_ffn_b2"][l], np.float32))

    x2, h2 = glue_host(x1, y, True)

    attn1 = attn_host("l1", h2, np.asarray(inputs["L1_attn_Wq"], np.float32),
                      np.asarray(inputs["L1_attn_Wk"], np.float32),
                      np.asarray(inputs["L1_attn_Wv"], np.float32))
    x3, h3 = glue_host(x2, attn1, True)

    ffn = l1ffn_host(h3, inputs)
    x4, _ = glue_host(x3, ffn, False)
    return x4
